# version 01
# baseline (speedup 1.0000x reference)
import math
import numpy as np
import concourse.bass as bass
import concourse.mybir as mybir
from concourse.bass_utils import run_bass_kernel_spmd

F32 = mybir.dt.float32
BF16 = mybir.dt.bfloat16
AF = mybir.ActivationFunctionType
ALU = mybir.AluOpType
AX = mybir.AxisListType.X

D = 1024
S = 2048
NCH = 8
H = 16
HD = 64
DL = 256
LAM = 0.01
EPS = 1e-5
QT = 512
NQ = S // QT

ENGINES = ("tensor", "vector", "scalar", "sync", "gpsimd")


class Prog:

    def __init__(self, nc):
        self.nc = nc
        self.q = {e: [] for e in ENGINES}
        self.handles = {}
        self.cnt = {}
        self.waited = {e: {} for e in ENGINES}
        self.dirty = {e: set() for e in ENGINES}
        self.reading = {e: set() for e in ENGINES}

    def add_sem(self, name, handle):
        self.handles[name] = handle
        self.cnt[name] = 0

    TRACKED = {"vector": "sV", "scalar": "sS"}

    @staticmethod
    def _ap_name(a):
        t = getattr(a, "tensor", None)
        if t is None:
            return None
        return getattr(t, "name", None) or str(t)

    def wait(self, eng, name, val=None, force=False):
        if val is None:
            val = self.cnt[name]
        if val <= 0:
            return
        assert val <= self.cnt[name], (eng, name, val, self.cnt[name])
        if not force and self.waited[eng].get(name, 0) >= val:
            return
        self.waited[eng][name] = max(self.waited[eng].get(name, 0), val)
        self.q[eng].append(("wait", name, val))
        if eng in self.TRACKED and name == self.TRACKED[eng] and val == self.cnt[name]:
            self.dirty[eng] = set()
            self.reading[eng] = set()

    def run(self, eng, method, *args, inc=None, incn=1, **kwargs):
        own = self.TRACKED.get(eng)
        if own is not None and method != "dma_start":
            if inc is None:
                inc = own
            names = []
            for a in list(args) + list(kwargs.values()):
                n = self._ap_name(a)
                if n is not None:
                    names.append(n)
            wname = names[0] if names else None
            rnames = set(names[1:])
            hazard = bool(rnames & self.dirty[eng]) or (
                wname is not None and (wname in self.dirty[eng] or wname in self.reading[eng]))
            if hazard and self.cnt[own] > 0:
                v = self.cnt[own]
                self.q[eng].append(("wait", own, v))
                self.waited[eng][own] = max(self.waited[eng].get(own, 0), v)
                self.dirty[eng] = set()
                self.reading[eng] = set()
            if wname is not None:
                self.dirty[eng].add(wname)
            self.reading[eng].update(rnames)
        tok = None
        if inc is not None:
            self.cnt[inc] += incn
            tok = self.cnt[inc]
            if self.waited[eng].get(inc, 0) < tok:
                self.waited[eng][inc] = tok
        self.q[eng].append(("call", method, args, kwargs, (inc, incn) if inc else None))
        return tok

    def replay(self, block):
        def mk(ename):
            def body(eng):
                for item in self.q[ename]:
                    if item[0] == "wait":
                        eng.wait_ge(self.handles[item[1]], item[2])
                    else:
                        _, m, args, kwargs, incinfo = item
                        ins = getattr(eng, m)(*args, **kwargs)
                        if incinfo is not None:
                            ins.then_inc(self.handles[incinfo[0]], incinfo[1])
            return body
        block.tensor(mk("tensor"))
        block.vector(mk("vector"))
        block.scalar(mk("scalar"))
        block.sync(mk("sync"))
        block.gpsimd(mk("gpsimd"))


def bcast_ap(row_ap, npart, width):
    from concourse.ap import AP
    return AP(row_ap.tensor, row_ap.offset, [[1, 1], [0, npart], [1, width]])


def build_program(taps=(), stage="full"):
    nc = bass.Bass("TRN2", target_bir_lowering=False, debug=False)
    taps = set(taps)

    xin = nc.dram_tensor("x", [D, S], F32, kind="ExternalInput")
    wshapes = {
        "wq": [D, D], "wk": [D, D], "wv": [D, D], "wo": [D, D],
        "wdn": [D, DL], "wuk": [DL, D], "wuv": [DL, D],
        "cw1": [D, D], "cw2": [D, 512], "cw3": [512, 256], "cw4": [256, 1],
        "gw1": [D, 512], "gw2": [512, 256], "gw3": [256, 1],
    }
    W = {k: nc.dram_tensor(k, v, F32, kind="ExternalInput") for k, v in wshapes.items()}
    bshapes = {"cb1": [D], "cb2": [512], "cb3": [256], "gb1": [512], "gb2": [256]}
    Bv = {k: nc.dram_tensor(k, v, F32, kind="ExternalInput") for k, v in bshapes.items()}
    Bs = {k: nc.dram_tensor(k, [1, 1], F32, kind="ExternalInput") for k in ("cb4", "gb3", "ar")}
    yout = nc.dram_tensor("y", [D, S], F32, kind="ExternalOutput")
    xspill = nc.dram_tensor("xspill", [D, S], F32, kind="Internal")

    tap_t = {}
    tshapes = {"x1": [128, NCH * S], "xln2": [128, NCH * S], "kv": [128, NCH * S],
               "latent": [128, 2 * S], "q": [128, NCH * S], "vrec": [128, 16 * 1056],
               "ao": [128, NCH * S], "x2": [128, NCH * S], "gf": [1, 16],
               "ab": [128, S], "bb": [128, S], "exp00": [128, 512], "nb": [2, 8],
               "S1": [128, NCH * QT], "gacc": [128, 8], "g1b": [128, 8], "g2b": [128, 4], "gaccb": [128, 16], "gpre": [128, 4]}
    for t in taps:
        tap_t[t] = nc.dram_tensor("tap_" + t, tshapes[t], F32, kind="ExternalOutput")

    tpos = np.arange(S, dtype=np.float64)
    u_neg = (np.exp(-LAM * tpos) / math.sqrt(32.0)).astype(np.float32)
    u_pos = (np.exp(+LAM * tpos) / math.sqrt(32.0)).astype(np.float32)
    UNEG_h = nc.inline_tensor(np.ascontiguousarray(np.tile(u_neg, (128, 1))), "c_uneg")
    UPOS_h = nc.inline_tensor(np.ascontiguousarray(np.tile(u_pos, (128, 1))), "c_upos")
    ii = np.arange(128, dtype=np.float64)
    DLOC_h = nc.inline_tensor(
        (np.exp(-LAM * np.abs(ii[:, None] - ii[None, :])) / 32.0).astype(np.float32), "c_dloc")
    ones2_np = np.zeros((128, 2), dtype=np.float32)
    ones2_np[0:64, 0] = 1.0
    ones2_np[64:128, 1] = 1.0
    ONES2_h = nc.inline_tensor(ones2_np, "c_ones2")

    A_X = 0
    A_RB = 65536
    A_CG = A_RB + 61440
    A_SM = A_CG + 38912
    arena_len = 204800

    ph = nc.alloc_sbuf_tensor("arena", [128, arena_len], mybir.dt.uint8)
    base = nc.lookup_mloc(ph).addr
    assert base + arena_len <= nc.sbuf_top, (base, arena_len, nc.sbuf_top)

    def at(name, off, shape, dtype):
        nbytes = int(np.prod(shape[1:])) * (2 if dtype == BF16 else 4)
        assert off + nbytes <= arena_len, (name, off, nbytes)
        return nc.alloc_sbuf_tensor_at(name, shape, dtype, offset=base + off)

    RB, CG, SM = A_RB, A_CG, A_SM
    x_sb = at("x_sb", A_X, [128, NCH * S], F32)
    uneg = at("uneg", SM + 0, [128, S], F32)
    upos = at("upos", SM + 8192, [128, S], F32)
    dloc = at("dloc", SM + 16384, [128, 128], F32)
    ones1b = at("ones1b", SM + 16896, [128, 1], BF16)
    ones2b = at("ones2b", SM + 16960, [128, 4], BF16)
    rowA = at("rowA", SM + 17024, [1, 1024], F32)
    rowB = at("rowB", SM + 21120, [1, 1024], F32)
    cb1 = at("cb1s", SM + 25216, [128, 8], F32)
    cb2 = at("cb2s", SM + 25248, [128, 4], F32)
    cb3 = at("cb3s", SM + 25280, [128, 2], F32)
    gb1 = at("gb1s", SM + 25312, [128, 4], F32)
    gb2 = at("gb2s", SM + 25344, [128, 2], F32)
    sm1 = at("sm1", SM + 25376, [1, 16], F32)
    smc = at("smc", SM + 25440, [128, 8], F32)
    gacc = at("gacc", SM + 25472, [128, 8], F32)
    gaccb = at("gaccb", SM + 25504, [128, 16], BF16)
    g1b = at("g1b", SM + 25536, [128, 8], BF16)
    g2b = at("g2b", SM + 25568, [128, 4], BF16)
    kn2 = at("kn2", SM + 25600, [2, 8], F32)
    epsc = at("epsc", SM + 25664, [1, 1], F32)
    onesF = at("onesF", SM + 25696, [1, 128], F32)
    rcpE0 = at("rcpE0", SM + 26240, [1, 512], F32)
    rcpO = at("rcpO", SM + 28288, [1, 512], F32)
    wuk_sb = at("wuk_sb", SM + 32384, [128, 2 * D], BF16)
    rcp_t = at("rcp_t", SM + 30336, [128, 512], F32)
    SM_CB4, SM_GB3, SM_AR, SM_GF, SM_A, SM_B, SM_T1, _u7, SM_QMAX, SM_LNQ, _u10, SM_T2 = range(12)
    SM_BSUM, SM_C = 12, 14

    ab_t = at("ab", RB + 0, [128, S], F32)
    bb_t = at("bb", RB + 8192, [128, S], F32)
    S_t = at("S_st", RB + 16384, [128, NCH * QT], F32)
    Sb_t = at("Sb_st", RB + 32768, [128, NCH * QT], BF16)
    h1b = at("h1b", RB + 40960, [128, NCH * QT], BF16)
    h2b = at("h2b", RB + 49152, [128, 4 * QT], BF16)
    h3b = at("h3b", RB + 53248, [128, 2 * QT], BF16)
    cbt = at("cbt", RB + 55296, [128, QT], F32)
    tmpA = at("tmpA", RB + 57344, [128, QT], F32)
    tmpB = at("tmpB", RB + 59392, [128, QT], F32)
    xbp = [at("xbp0", RB + 32768, [128, S], BF16),
           at("xbp1", RB + 36864, [128, S], BF16)]
    sqp = [at("sqp0", RB + 49152, [128, S], BF16),
           at("sqp1", RB + 53248, [128, S], BF16)]
    gsc = at("gsc", RB + 40960, [128, S], F32)

    xln2b = at("xln2b", RB + 16384, [128, NCH * S], BF16)
    vrec = at("vrec", RB + 0, [128, 16 * 1056], BF16)
    wo_sb = at("wo_sb", RB + 34304, [128, NCH * D], BF16)
    xB_t = at("xB", RB + 0, [128, 4 * S], F32)

    kv_sb = at("kv_sb", A_X + 0, [128, NCH * S], BF16)
    q_sb = at("q_sb", A_X + 0, [128, NCH * S], BF16)
    ao_sb = at("ao_sb", A_X + 32768, [128, NCH * S], BF16)
    wk_sb = at("wk_sb", A_X + 32768, [128, NCH * D], BF16)
    wv_sb = at("wv_sb", A_X + 49152, [128, NCH * D], BF16)
    wq_sb = at("wq_sb", A_X + 32768, [128, NCH * D], BF16)
    wdn_sb = at("wdn_sb", A_X + 49152, [128, 8 * 256], BF16)
    wuv_sb = at("wuv_sb", A_X + 53248, [128, 2 * D], BF16)
    xst = at("xst", CG + 0, [128, S], F32)

    cw1 = at("cw1s", CG + 0, [128, 8 * D], BF16)
    cw2 = at("cw2s", CG + 16384, [128, 8 * 512], BF16)
    cw3 = at("cw3s", CG + 24576, [128, 4 * 256], BF16)
    cw4 = at("cw4s", CG + 26624, [128, 4], BF16)
    gw1 = at("gw1s", CG + 26688, [128, 8 * 512], BF16)
    gw2 = at("gw2s", CG + 34880, [128, 4 * 256], BF16)
    gw3 = at("gw3s", CG + 36928, [128, 4], BF16)

    latent = at("latent", CG + 0, [128, 2 * S], BF16)
    ktil = at("ktil", CG + 8192, [128, S], BF16)
    khat = at("khat", CG + 12288, [128, S], BF16)
    qtil = at("qtil", CG + 16384, [128, S], BF16)
    qhat = at("qhat", CG + 20480, [128, S], BF16)
    kraw = at("kraw", CG + 24576, [128, S], BF16)
    expt = [at(f"expt{i}", CG + 28672 + i * 1024, [128, 512], BF16) for i in range(4)]
    rb_t = at("rb_t", CG + 32768, [128, 1024], F32)
    ksq_t = at("ksq", CG + 36864, [128, 512], BF16)
    qsq_t = [at("qsq0", CG + 8192, [128, 512], BF16),
             at("qsq1", CG + 12288, [128, 512], BF16)]

    psA = nc.alloc_psum_tensor("psA", [128, 4096], F32)

    from contextlib import ExitStack
    ctx = ExitStack()
    p = Prog(nc)
    for sname in ("sT", "sV", "sS", "qS", "qG"):
        p.add_sem(sname, ctx.enter_context(nc.semaphore(sname)))

    def dma(eng_q, out, in_, slow=False):
        sem = "qS" if eng_q == "sync" else "qG"
        kw = {"allow_slow_non_contiguous": True} if slow else {}
        return p.run(eng_q, "dma_start", inc=sem, incn=16, out=out, in_=in_, **kw)

    def bc(row_ap, dst_ap, width, npart=128):
        return dma("sync", dst_ap, bcast_ap(row_ap, npart, width))

    def bc_scalar(cell_ap, dst_col, pscol):
        p.wait("tensor", "sV")
        p.run("tensor", "matmul", psA[:, 3584 + pscol:3584 + pscol + 1], onesF[:], cell_ap,
              start=True, stop=True, inc="sT")
        p.wait("vector", "sT")
        p.run("vector", "tensor_copy", dst_col, psA[:, 3584 + pscol:3584 + pscol + 1], inc="sV")
        return p.cnt["sV"]

    def xc(c):
        return x_sb[:, c * S:(c + 1) * S]

    def load_weight(dst, hbm, rows, slow=False):
        nchk = rows // 128
        src = hbm.ap().rearrange("(c p) o -> p c o", p=128)
        return dma("gpsimd", dst[:].rearrange("p (c o) -> p c o", c=nchk), src, slow=slow)

    def col2(tile, i):
        return tile[:, 2 * i:2 * i + 1]

    def wslice(wsb, ow, fi, fo, w=128):
        return wsb[:, fi * ow + fo * w:fi * ow + fo * w + w]

    def tap(name, src_ap, bf=False):
        if name not in tap_t:
            return
        q = "gpsimd" if bf else "sync"
        for sname2 in ("sT", "sV", "sS"):
            p.wait(q, sname2)
        dma(q, tap_t[name].ap(), src_ap)

    dma("sync", x_sb[:].rearrange("p (c t) -> p c t", c=NCH),
        xin.ap().rearrange("(c p) t -> p c t", p=128))
    t_x_loaded = p.cnt["qS"]
    dma("sync", uneg[:], UNEG_h.ap())
    dma("sync", upos[:], UPOS_h.ap())
    dma("sync", dloc[:], DLOC_h.ap())
    dma("gpsimd", ones2b[:].rearrange("p (c o) -> p c o", o=2)[:, :, 0:1], ONES2_h.ap().rearrange("p (c o) -> p c o", o=1), slow=True)
    p.run("vector", "memset", ones1b[:], 1.0, inc="sV")
    p.run("vector", "memset", epsc[:], EPS, inc="sV")
    p.run("vector", "memset", onesF[:], 1.0, inc="sV")
    dma("sync", cb1[:], Bv["cb1"].ap().rearrange("(c p) -> p c", p=128), slow=True)
    dma("sync", cb2[:], Bv["cb2"].ap().rearrange("(c p) -> p c", p=128), slow=True)
    dma("sync", cb3[:], Bv["cb3"].ap().rearrange("(c p) -> p c", p=128), slow=True)
    dma("sync", gb1[:], Bv["gb1"].ap().rearrange("(c p) -> p c", p=128), slow=True)
    dma("sync", gb2[:], Bv["gb2"].ap().rearrange("(c p) -> p c", p=128), slow=True)
    dma("sync", sm1[0:1, SM_CB4:SM_CB4 + 1], Bs["cb4"].ap())
    dma("sync", sm1[0:1, SM_GB3:SM_GB3 + 1], Bs["gb3"].ap())
    dma("sync", sm1[0:1, SM_AR:SM_AR + 1], Bs["ar"].ap())
    t_bias = p.cnt["qS"]

    def load_cngc():
        load_weight(cw1, W["cw1"], D)
        load_weight(cw2, W["cw2"], D)
        load_weight(cw3, W["cw3"], 512)
        dma("gpsimd", cw4[:].rearrange("p (c o) -> p c o", o=2)[:, :, 0:1], W["cw4"].ap().rearrange("(c p) o -> p c o", p=128), slow=True)
        load_weight(gw1, W["gw1"], D)
        load_weight(gw2, W["gw2"], 512)
        dma("gpsimd", gw3[:].rearrange("p (c o) -> p c o", o=2)[:, :, 0:1], W["gw3"].ap().rearrange("(c p) o -> p c o", p=128), slow=True)
        return p.cnt["qG"]
    t_cngc1 = load_cngc()

    def emit_stats(xap, wait_toks):
        stat_tok = {}
        for c in range(NCH):
            xb = xbp[c % 2]
            sq = sqp[c % 2]
            for sem, val in wait_toks:
                p.wait("vector", sem, val)
                p.wait("scalar", sem, val)
                p.wait("tensor", sem, val)
            if c >= 2:
                p.wait("vector", "sT", stat_tok[c - 2])
                p.wait("scalar", "sT", stat_tok[c - 2])
            p.run("vector", "tensor_copy", xb[:], xap(c), inc="sV")
            tv = p.cnt["sV"]
            p.run("scalar", "activation", sq[:], xap(c), AF.Square, inc="sS")
            ts = p.cnt["sS"]
            p.wait("tensor", "sV", tv)
            p.wait("tensor", "sS", ts)
            for j in range(4):
                p.run("tensor", "matmul", psA[0:1, j * 512:(j + 1) * 512], ones1b[:],
                      xb[:, j * 512:(j + 1) * 512], start=(c == 0), stop=(c == NCH - 1))
                p.run("tensor", "matmul", psA[0:1, 2048 + j * 512:2048 + (j + 1) * 512], ones1b[:],
                      sq[:, j * 512:(j + 1) * 512], start=(c == 0), stop=(c == NCH - 1),
                      inc="sT" if j == 3 else None)
            stat_tok[c] = p.cnt["sT"]
        return p.cnt["sT"]

    def emit_rows_ab(t_stats):
        for hh in range(2):
            hs = hh * 1024
            p.wait("vector", "sT", t_stats)
            p.wait("vector", "qS")
            p.run("vector", "tensor_scalar", rowA[:], psA[0:1, hs:hs + 1024], 1.0 / D, None,
                  op0=ALU.mult, inc="sV")
            p.wait("scalar", "sV")
            p.run("scalar", "activation", rowB[:], rowA[:], AF.Square, inc="sS")
            p.wait("vector", "sS")
            p.run("vector", "scalar_tensor_tensor", rowB[:], psA[0:1, 2048 + hs:2048 + hs + 1024],
                  1.0 / D, rowB[:], op0=ALU.mult, op1=ALU.subtract, inc="sV")
            p.wait("scalar", "sV")
            p.run("scalar", "activation", rowB[:], rowB[:], AF.Sqrt, bias=epsc[0:1, 0:1], inc="sS")
            p.wait("vector", "sS")
            p.run("vector", "reciprocal", rowA[0:1, 0:1024], rowB[:], inc="sV")
            p.run("vector", "tensor_copy", rowB[:], rowA[0:1, 0:1024], inc="sV")
            p.run("vector", "tensor_scalar", rowA[:], psA[0:1, hs:hs + 1024], 1.0 / D, None,
                  op0=ALU.mult, inc="sV")
            p.run("vector", "scalar_tensor_tensor", rowA[:], rowA[:], -1.0, rowB[:],
                  op0=ALU.mult, op1=ALU.mult, inc="sV")
            trow = p.cnt["sV"]
            p.wait("sync", "sV", trow)
            bc(rowB[0:1, :], ab_t[:, hs:hs + 1024], 1024)
            bc(rowA[0:1, :], bb_t[:, hs:hs + 1024], 1024)
        return p.cnt["qS"]

    def emit_ffn(xap, wait_toks, t_w):
        t_stats = emit_stats(xap, wait_toks)
        for sem, val in wait_toks:
            p.wait("sync", sem, val)
        t_ab = emit_rows_ab(t_stats)
        tap("ab", ab_t[:])
        tap("bb", bb_t[:])
        p.wait("vector", "qS", t_ab)
        for c in range(NCH):
            p.run("vector", "tensor_mul", gsc[:], xap(c), ab_t[:], inc="sV")
            p.run("vector", "reduce_sum", gacc[:, c:c + 1], gsc[:], AX, inc="sV")
        p.run("vector", "reduce_sum", sm1[0:1, SM_BSUM:SM_BSUM + 1], bb_t[0:1, :], AX, inc="sV")
        bc_scalar(sm1[0:1, SM_BSUM:SM_BSUM + 1], smc[:, 0:1], 0)
        p.run("vector", "tensor_scalar", gacc[:], gacc[:], smc[:, 0:1], 1.0 / S,
              op0=ALU.add, op1=ALU.mult, inc="sV")
        p.run("vector", "tensor_copy", gaccb[:].rearrange("p (c o) -> p c o", o=2)[:, :, 0:1], gacc[:].rearrange("p (c o) -> p c o", o=1), inc="sV")
        tg = p.cnt["sV"]
        tap("gacc", gacc[:])
        p.wait("tensor", "sV", tg)
        p.wait("tensor", "sS")
        p.wait("tensor", "qG", t_w)
        for fo in range(4):
            for fi in range(NCH):
                p.run("tensor", "matmul", psA[:, fo:fo + 1], wslice(gw1, 512, fi, fo),
                      col2(gaccb, fi), start=(fi == 0), stop=(fi == NCH - 1),
                      inc="sT" if fi == NCH - 1 else None)
        p.wait("scalar", "sT")
        p.wait("scalar", "qS", t_bias)
        if "gpre" in tap_t:
            p.wait("vector", "sT")
            p.run("vector", "tensor_copy", gacc[:, 0:4], psA[:, 0:4], inc="sV")
            tap("gpre", gacc[:, 0:4])
        for fo in range(4):
            p.run("scalar", "activation", col2(g1b, fo), psA[:, fo:fo + 1], AF.Gelu,
                  bias=gb1[:, fo:fo + 1], inc="sS")
        p.wait("tensor", "sS")
        for fo in range(2):
            for fi in range(4):
                p.run("tensor", "matmul", psA[:, 512 + fo:512 + fo + 1], wslice(gw2, 256, fi, fo),
                      col2(g1b, fi), start=(fi == 0), stop=(fi == 3),
                      inc="sT" if fi == 3 else None)
        p.wait("scalar", "sT")
        for fo in range(2):
            p.run("scalar", "activation", col2(g2b, fo), psA[:, 512 + fo:512 + fo + 1], AF.Gelu,
                  bias=gb2[:, fo:fo + 1], inc="sS")
        p.wait("tensor", "sS")
        for fi in range(2):
            p.run("tensor", "matmul", psA[0:1, 1024:1025], col2(gw3, fi), col2(g2b, fi),
                  start=(fi == 0), stop=(fi == 1), inc="sT" if fi == 1 else None)
        p.wait("scalar", "sT")
        p.run("scalar", "activation", sm1[0:1, SM_GF:SM_GF + 1], psA[0:1, 1024:1025], AF.Sigmoid,
              bias=sm1[0:1, SM_GB3:SM_GB3 + 1], inc="sS")
        p.wait("vector", "sS")
        ar_ap = sm1[0:1, SM_AR:SM_AR + 1]
        p.run("vector", "tensor_scalar", sm1[0:1, SM_T2:SM_T2 + 1], sm1[0:1, SM_GF:SM_GF + 1],
              ar_ap, None, op0=ALU.mult, inc="sV")
        p.run("vector", "tensor_scalar", sm1[0:1, SM_T1:SM_T1 + 1], ar_ap,
              -1.0, 0.1, op0=ALU.mult, op1=ALU.add, inc="sV")
        p.run("vector", "tensor_scalar", sm1[0:1, SM_A:SM_A + 1], sm1[0:1, SM_T2:SM_T2 + 1],
              1.4, sm1[0:1, SM_T1:SM_T1 + 1], op0=ALU.mult, op1=ALU.add, inc="sV")
        p.run("vector", "tensor_scalar", sm1[0:1, SM_B:SM_B + 1], ar_ap,
              0.6, None, op0=ALU.mult, inc="sV")
        t_AB = p.cnt["sV"]
        tap("gf", sm1[:])
        tap("gaccb", gaccb[:], bf=True)
        tap("g1b", g1b[:], bf=True)
        tap("g2b", g2b[:], bf=True)

        SIGN = {0: (+1, -1), 1: (+1, -1), 2: (-1, +1), 3: (-1, +1)}
        t_S = p.cnt["sV"]
        prev_cb = 0
        for qq in range(NQ):
            qs = qq * QT
            p.wait("vector", "sT")
            p.wait("vector", "qS", t_ab)
            for c in range(NCH):
                xsl = xap(c)[:, qs:qs + QT]
                Ssl = S_t[:, c * QT:(c + 1) * QT]
                p.run("vector", "tensor_mul", Ssl, xsl, ab_t[:, qs:qs + QT], inc="sV")
                p.run("vector", "tensor_add", Ssl, Ssl, bb_t[:, qs:qs + QT], inc="sV")
                p.run("vector", "tensor_copy", Sb_t[:, c * QT:(c + 1) * QT], Ssl, inc="sV")
            t_S = p.cnt["sV"]
            if qq == 0:
                tap("S1", S_t[:])
            for cyc in range(3):
                p.wait("tensor", "sV", t_S)
                p.wait("tensor", "sS")
                h_evac = {}
                for fo in range(NCH):
                    bank = fo % 2
                    if fo >= 2:
                        p.wait("tensor", "sS", h_evac[fo - 2])
                    for fi in range(NCH):
                        p.run("tensor", "matmul", psA[:, bank * 512:(bank + 1) * 512],
                              wslice(cw1, D, fi, fo), Sb_t[:, fi * QT:(fi + 1) * QT],
                              start=(fi == 0), stop=(fi == NCH - 1),
                              inc="sT" if fi == NCH - 1 else None)
                    p.wait("scalar", "sT")
                    p.run("scalar", "activation", h1b[:, fo * QT:(fo + 1) * QT],
                          psA[:, bank * 512:(bank + 1) * 512], AF.Gelu,
                          bias=cb1[:, fo:fo + 1], inc="sS")
                    h_evac[fo] = p.cnt["sS"]
                for fo in range(4):
                    bank = 2 + fo % 2
                    if fo >= 2:
                        p.wait("tensor", "sS", h_evac[NCH + fo - 2])
                    for fi in range(NCH):
                        p.wait("tensor", "sS", h_evac[fi])
                        p.run("tensor", "matmul", psA[:, bank * 512:(bank + 1) * 512],
                              wslice(cw2, 512, fi, fo), h1b[:, fi * QT:(fi + 1) * QT],
                              start=(fi == 0), stop=(fi == NCH - 1),
                              inc="sT" if fi == NCH - 1 else None)
                    p.wait("scalar", "sT")
                    p.run("scalar", "activation", h2b[:, fo * QT:(fo + 1) * QT],
                          psA[:, bank * 512:(bank + 1) * 512], AF.Gelu,
                          bias=cb2[:, fo:fo + 1], inc="sS")
                    h_evac[NCH + fo] = p.cnt["sS"]
                for fo in range(2):
                    bank = 4 + fo
                    for fi in range(4):
                        p.wait("tensor", "sS", h_evac[NCH + fi])
                        p.run("tensor", "matmul", psA[:, bank * 512:(bank + 1) * 512],
                              wslice(cw3, 256, fi, fo), h2b[:, fi * QT:(fi + 1) * QT],
                              start=(fi == 0), stop=(fi == 3),
                              inc="sT" if fi == 3 else None)
                    p.wait("scalar", "sT")
                    p.run("scalar", "activation", h3b[:, fo * QT:(fo + 1) * QT],
                          psA[:, bank * 512:(bank + 1) * 512], AF.Gelu,
                          bias=cb3[:, fo:fo + 1], inc="sS")
                    h_evac[12 + fo] = p.cnt["sS"]
                for fi in range(2):
                    p.wait("tensor", "sS", h_evac[12 + fi])
                    p.run("tensor", "matmul", psA[0:1, 3072:3584], col2(cw4, fi),
                          h3b[:, fi * QT:(fi + 1) * QT], start=(fi == 0), stop=(fi == 1),
                          inc="sT" if fi == 1 else None)
                tz = p.cnt["sT"]
                p.wait("scalar", "sT", tz)
                p.wait("scalar", "sV")
                p.run("scalar", "activation", rowA[0:1, 0:512], psA[0:1, 3072:3584], AF.Sigmoid,
                      inc="sS")
                p.wait("vector", "sS")
                p.wait("vector", "sV", t_AB)
                p.wait("vector", "qS", prev_cb)
                p.run("vector", "tensor_scalar", rowB[0:1, 0:512], rowA[0:1, 0:512],
                      sm1[0:1, SM_B:SM_B + 1], sm1[0:1, SM_A:SM_A + 1],
                      op0=ALU.mult, op1=ALU.add, inc="sV")

                tc_row = p.cnt["sV"]
                p.wait("sync", "sV", tc_row)
                prev_cb = bc(rowB[0:1, 0:512], cbt[:], 512)
                p.wait("vector", "qS", prev_cb)
                for a in range(4):
                    b = a + 4
                    sa, sb = SIGN[a]
                    Sa = S_t[:, a * QT:(a + 1) * QT]
                    Sb_ = S_t[:, b * QT:(b + 1) * QT]
                    p.run("vector", "tensor_mul", tmpA[:], Sb_, cbt[:], inc="sV")
                    p.run("vector", "tensor_mul", tmpB[:], Sa, cbt[:], inc="sV")
                    p.run("vector", "tensor_tensor", Sa, Sa, tmpA[:],
                          op=(ALU.add if sa > 0 else ALU.subtract), inc="sV")
                    p.run("vector", "tensor_tensor", Sb_, Sb_, tmpB[:],
                          op=(ALU.add if sb > 0 else ALU.subtract), inc="sV")
                    if cyc < 2:
                        p.run("vector", "tensor_copy", Sb_t[:, a * QT:(a + 1) * QT], Sa, inc="sV")
                        p.run("vector", "tensor_copy", Sb_t[:, b * QT:(b + 1) * QT], Sb_, inc="sV")
                t_S = p.cnt["sV"]
            for c in range(NCH):
                p.run("vector", "scalar_tensor_tensor", xap(c)[:, qs:qs + QT],
                      S_t[:, c * QT:(c + 1) * QT], 0.5, xap(c)[:, qs:qs + QT],
                      op0=ALU.mult, op1=ALU.add, inc="sV")
        return p.cnt["sV"]

    def finalize():
        p.wait("sync", "sV")
        p.wait("sync", "sS")
        p.wait("sync", "sT")
        dma("sync", yout.ap().rearrange("(c p) t -> p c t", p=128),
            x_sb[:].rearrange("p (c t) -> p c t", c=NCH))
        p.wait("sync", "qS")
        p.wait("sync", "qG")
        with nc.Block() as block:
            p.replay(block)
        ctx.close()
        return nc

    if stage == "load":
        return finalize()

    t_x1 = emit_ffn(xc, [("qS", t_x_loaded)], t_cngc1)
    tap("x1", x_sb[:])
    if stage == "ffn1":
        return finalize()

    t2_stats = emit_stats(xc, [("sV", t_x1)])
    t2_ab = emit_rows_ab(t2_stats)
    p.wait("vector", "qS", t2_ab)
    for c in range(NCH):
        p.run("vector", "tensor_mul", xst[:], xc(c), ab_t[:], inc="sV")
        p.run("vector", "tensor_add", xln2b[:, c * S:(c + 1) * S], xst[:], bb_t[:], inc="sV")
    t_xln2 = p.cnt["sV"]
    tap("xln2", xln2b[:], bf=True)
    if stage == "xln2":
        return finalize()
    p.wait("sync", "sV", t_xln2)
    xsp = xspill.ap().rearrange("(c p) t -> p c t", p=128)
    dma("sync", xsp[:, 4:8, :], x_sb[:, 4 * S:].rearrange("p (c t) -> p c t", c=4))
    t_spill_hi = p.cnt["qS"]
    dma("sync", xsp[:, 0:4, :], x_sb[:, 0:4 * S].rearrange("p (c t) -> p c t", c=4))
    t_spill_lo = p.cnt["qS"]

    p.wait("gpsimd", "qS", t_spill_hi)
    load_weight(wk_sb, W["wk"], D)
    load_weight(wv_sb, W["wv"], D)
    t_wkv = p.cnt["qG"]

    kv_evac = {}
    for fo in range(NCH):
        for n in range(4):
            idx = fo * 4 + n
            bank = idx % 2
            if idx >= 2:
                p.wait("tensor", "sS", kv_evac[idx - 2])
            p.wait("tensor", "sV", t_xln2)
            p.wait("tensor", "qG", t_wkv)
            p.wait("tensor", "qS", t_spill_lo)
            for fi in range(NCH):
                p.run("tensor", "matmul", psA[:, bank * 512:(bank + 1) * 512],
                      wslice(wk_sb, D, fi, fo), xln2b[:, fi * S + n * 512:fi * S + (n + 1) * 512],
                      start=(fi == 0), stop=False)
            for fi in range(NCH):
                p.run("tensor", "matmul", psA[:, bank * 512:(bank + 1) * 512],
                      wslice(wv_sb, D, fi, fo), xln2b[:, fi * S + n * 512:fi * S + (n + 1) * 512],
                      start=False, stop=(fi == NCH - 1), inc="sT" if fi == NCH - 1 else None)
            p.wait("scalar", "sT")
            p.run("scalar", "activation", kv_sb[:, fo * S + n * 512:fo * S + (n + 1) * 512],
                  psA[:, bank * 512:(bank + 1) * 512], AF.Copy, inc="sS")
            kv_evac[idx] = p.cnt["sS"]
    t_kv = p.cnt["sS"]
    tap("kv", kv_sb[:], bf=True)
    if stage == "kv":
        return finalize()

    p.wait("gpsimd", "sT")
    load_weight(wdn_sb, W["wdn"], D)
    t_wdn = p.cnt["qG"]
    lat_evac = {}
    for fo in range(2):
        for n in range(4):
            idx = fo * 4 + n
            bank = idx % 2
            if idx >= 2:
                p.wait("tensor", "sS", lat_evac[idx - 2])
            p.wait("tensor", "sS", t_kv)
            p.wait("tensor", "qG", t_wdn)
            for fi in range(NCH):
                p.run("tensor", "matmul", psA[:, bank * 512:(bank + 1) * 512],
                      wslice(wdn_sb, DL, fi, fo), kv_sb[:, fi * S + n * 512:fi * S + (n + 1) * 512],
                      start=(fi == 0), stop=(fi == NCH - 1), inc="sT" if fi == NCH - 1 else None)
            p.wait("scalar", "sT")
            p.run("scalar", "activation", latent[:, fo * S + n * 512:fo * S + (n + 1) * 512],
                  psA[:, bank * 512:(bank + 1) * 512], AF.Copy, inc="sS")
            lat_evac[idx] = p.cnt["sS"]
    t_latent = p.cnt["sS"]
    tap("latent", latent[:], bf=True)
    if stage == "latent":
        return finalize()

    p.wait("gpsimd", "sT")
    load_weight(wq_sb, W["wq"], D)
    t_wq = p.cnt["qG"]
    q_evac, q_sq_s, q_sq_t = {}, {}, {}
    for fo in range(NCH):
        for n in range(4):
            idx = fo * 4 + n
            bank = idx % 2
            if idx >= 2:
                p.wait("tensor", "sV", q_evac[idx - 2])
            p.wait("tensor", "sS", t_latent)
            p.wait("tensor", "qG", t_wq)
            for fi in range(NCH):
                p.run("tensor", "matmul", psA[:, bank * 512:(bank + 1) * 512],
                      wslice(wq_sb, D, fi, fo), xln2b[:, fi * S + n * 512:fi * S + (n + 1) * 512],
                      start=(fi == 0), stop=(fi == NCH - 1), inc="sT" if fi == NCH - 1 else None)
            tmm = p.cnt["sT"]
            p.wait("vector", "sT", tmm)
            p.run("vector", "tensor_copy", q_sb[:, fo * S + n * 512:fo * S + (n + 1) * 512],
                  psA[:, bank * 512:(bank + 1) * 512], inc="sV")
            q_evac[idx] = p.cnt["sV"]
            if stage == "qnn":
                q_sq_s[idx] = p.cnt["sS"]
                q_sq_t[idx] = p.cnt["sT"]
                continue
            if idx >= 2:
                p.wait("vector", "sT", q_sq_t[idx - 2])
            qsq = qsq_t[bank]
            p.run("vector", "tensor_mul", qsq[:],
                  q_sb[:, fo * S + n * 512:fo * S + (n + 1) * 512],
                  q_sb[:, fo * S + n * 512:fo * S + (n + 1) * 512], inc="sV")
            q_sq_s[idx] = p.cnt["sV"]
            p.wait("tensor", "sV", q_sq_s[idx])
            p.run("tensor", "matmul", psA[0:1, 2048 + n * 512:2048 + (n + 1) * 512], ones1b[:],
                  qsq[:], start=(fo == 0), stop=(fo == NCH - 1), inc="sT")
            q_sq_t[idx] = p.cnt["sT"]
    t_q = p.cnt["sV"]
    t_qn = p.cnt["sT"]
    tap("q", q_sb[:], bf=True)
    if stage == "qnn":
        return finalize()
    if stage == "q":
        return finalize()
    p.wait("vector", "sT", t_qn)
    p.run("vector", "reduce_max", sm1[0:1, SM_QMAX:SM_QMAX + 1], psA[0:1, 2048:4096], AX, inc="sV")
    p.wait("scalar", "sV")
    p.run("scalar", "activation", sm1[0:1, SM_LNQ:SM_LNQ + 1], sm1[0:1, SM_QMAX:SM_QMAX + 1],
          AF.Ln, inc="sS")
    p.wait("scalar", "sS", force=True)
    p.wait("vector", "sS")
    p.run("vector", "tensor_scalar", sm1[0:1, SM_C:SM_C + 1], sm1[0:1, SM_LNQ:SM_LNQ + 1],
          0.5, -0.5 * math.log(32.0), op0=ALU.mult, op1=ALU.add, inc="sV")
    t_C = p.cnt["sV"]

    p.wait("gpsimd", "sT")
    load_weight(wuv_sb, W["wuv"], DL)
    t_wuv = p.cnt["qG"]
    p.wait("vector", "sV", t_q)
    p.wait("vector", "sT", t_qn)
    vr_view = vrec[:].rearrange("p (a w) -> p a w", w=66)
    p.run("vector", "memset", vr_view[:, :, 64:65], 1.0, inc="sV")
    vr_evac = {}
    for tc in range(16):
        for fh in range(2):
            idx = tc * 2 + fh
            bank = idx % 2
            if idx >= 2:
                p.wait("tensor", "sV", vr_evac[idx - 2])
            p.wait("tensor", "sS", t_latent)
            p.wait("tensor", "sV", t_q)
            p.wait("tensor", "sT", t_qn)
            p.wait("tensor", "qG", t_wuv)
            for d in range(2):
                p.run("tensor", "matmul", psA[:, bank * 512:(bank + 1) * 512],
                      latent[:, d * S + tc * 128:d * S + (tc + 1) * 128],
                      wuv_sb[:, d * D + fh * 512:d * D + (fh + 1) * 512],
                      start=(d == 0), stop=(d == 1), inc="sT" if d == 1 else None)
            p.wait("vector", "sT")
            dst = vrec[:, tc * 1056 + fh * 528:tc * 1056 + (fh + 1) * 528].rearrange(
                "p (h w) -> p h w", w=66)[:, :, 0:64]
            p.run("vector", "tensor_copy", dst,
                  psA[:, bank * 512:(bank + 1) * 512].rearrange("p (h w) -> p h w", w=64),
                  inc="sV")
            vr_evac[idx] = p.cnt["sV"]
    t_vrec = p.cnt["sV"]
    tap("vrec", vrec[:], bf=True)
    if stage == "vrec":
        return finalize()

    p.wait("gpsimd", "sV", t_q)
    load_weight(wuk_sb, W["wuk"], DL)
    t_wuk = p.cnt["qG"]
    load_weight(wo_sb, W["wo"], D)
    t_wo = p.cnt["qG"]

    av_clear = 0
    npairs = 1 if stage.startswith("p0") or stage == "pair0" else 8
    for pair in range(npairs):
        he, ho = 2 * pair, 2 * pair + 1
        p.wait("tensor", "qG", t_wuk)
        p.wait("tensor", "sS", t_latent)
        p.wait("tensor", "sV", t_vrec)
        p.wait("tensor", "sS")
        p.wait("tensor", "sV", av_clear)
        for n in range(4):
            for d in range(2):
                p.run("tensor", "matmul", psA[:, n * 512:(n + 1) * 512],
                      wuk_sb[:, d * D + pair * 128:d * D + (pair + 1) * 128],
                      latent[:, d * S + n * 512:d * S + (n + 1) * 512],
                      start=(d == 0), stop=(d == 1), inc="sT" if d == 1 else None)
        t_krec = p.cnt["sT"]
        p.wait("vector", "sT", t_krec)
        p.run("vector", "tensor_mul", ktil[:], psA[:, 0:2048], upos[:], inc="sV")
        p.run("vector", "tensor_mul", khat[:], psA[:, 0:2048], uneg[:], inc="sV")
        p.run("vector", "tensor_copy", kraw[:], psA[:, 0:2048], inc="sV")
        t_ktiles = p.cnt["sV"]
        if stage == "p0k":
            break
        for n in range(4):
            p.wait("vector", "sT")
            p.run("vector", "tensor_mul", ksq_t[:], ktil[:, n * 512:(n + 1) * 512],
                  khat[:, n * 512:(n + 1) * 512], inc="sV")
            tk = p.cnt["sV"]
            p.wait("tensor", "sV", tk)
            p.run("tensor", "matmul", psA[0:1, 2048 + n * 512:2048 + (n + 1) * 512],
                  col2(ones2b, 0), ksq_t[:], start=True, stop=True)
            p.run("tensor", "matmul", psA[0:1, n * 512:(n + 1) * 512],
                  col2(ones2b, 1), ksq_t[:], start=True, stop=True, inc="sT")
        t_kn = p.cnt["sT"]
        if stage == "p0n":
            break
        p.wait("vector", "sT", t_kn)
        p.run("vector", "reduce_max", kn2[0:1, 0:1], psA[0:1, 2048:4096], AX, inc="sV")
        p.run("vector", "reduce_max", kn2[0:1, 1:2], psA[0:1, 0:2048], AX, inc="sV")
        p.wait("scalar", "sV")
        p.wait("scalar", "sV", t_C)
        p.run("scalar", "activation", kn2[0:1, 2:4], kn2[0:1, 0:2], AF.Ln, inc="sS")
        p.wait("scalar", "sS", force=True)
        p.run("scalar", "activation", kn2[0:1, 4:6], kn2[0:1, 2:4], AF.Exp, scale=0.5,
              bias=sm1[0:1, SM_C:SM_C + 1], inc="sS")
        p.wait("vector", "sS")
        p.run("vector", "tensor_scalar", kn2[0:1, 6:8], kn2[0:1, 4:6], -1.0, None,
              op0=ALU.mult, inc="sV")
        p.wait("sync", "sS")
        if stage == "p0r":
            break
        bc_scalar(kn2[0:1, 6:7], smc[:, 4:5], 0)
        bc_scalar(kn2[0:1, 7:8], smc[:, 6:7], 1)
        t_negB = p.cnt["sV"]
        if pair == 0:
            tap("nb", kn2[:])
        p.wait("vector", "sT")
        p.run("vector", "tensor_mul", qtil[:], q_sb[:, pair * S:(pair + 1) * S], uneg[:], inc="sV")
        p.run("vector", "tensor_mul", qhat[:], q_sb[:, pair * S:(pair + 1) * S], upos[:], inc="sV")
        t_qtiles = p.cnt["sV"]
        if stage == "p0a":
            break

        for qc in range(4):
            qlo = qc * 512
            exp_tok = {}
            av_done = {}
            for kc in range(16):
                sb_e = (kc % 2) * 2
                sb_o = sb_e + 1
                if kc >= 2:
                    p.wait("tensor", "sS", exp_tok[kc - 2])
                p.wait("tensor", "sV", t_qtiles)
                if kc == 0:
                    p.wait("tensor", "sV", av_clear)
                dlo, dhi = kc * 128, kc * 128 + 128
                for base, sb in ((0, sb_e), (64, sb_o)):
                    bo = sb * 512
                    pr = slice(base, base + 64)
                    segs = []
                    lo, hi = qlo, min(qlo + 512, dlo)
                    if hi > lo:
                        segs.append((lo, hi, "hat"))
                    lo, hi = max(qlo, dlo), min(qlo + 512, dhi)
                    if hi > lo:
                        segs.append((lo, hi, "diag"))
                    lo, hi = max(qlo, dhi), qlo + 512
                    if hi > lo:
                        segs.append((lo, hi, "til"))
                    for si, (lo, hi, kind) in enumerate(segs):
                        if kind == "hat":
                            lh = khat[pr, kc * 128:(kc + 1) * 128]
                            rh = qhat[pr, lo:hi]
                        elif kind == "til":
                            lh = ktil[pr, kc * 128:(kc + 1) * 128]
                            rh = qtil[pr, lo:hi]
                        else:
                            lh = kraw[pr, kc * 128:(kc + 1) * 128]
                            rh = q_sb[pr, pair * S + lo:pair * S + hi]
                        p.run("tensor", "matmul", psA[:, bo + (lo - qlo):bo + (hi - qlo)], lh, rh,
                              start=True, stop=True, tile_position=(base, 0),
                              inc="sT" if si == len(segs) - 1 else None)
                t_sc = p.cnt["sT"]
                if kc // 4 == qc:
                    dof = kc * 128 - qlo
                    p.wait("vector", "sT", t_sc)
                    p.run("vector", "tensor_mul", psA[:, sb_e * 512 + dof:sb_e * 512 + dof + 128],
                          psA[:, sb_e * 512 + dof:sb_e * 512 + dof + 128], dloc[:], inc="sV")
                    p.run("vector", "tensor_mul", psA[:, sb_o * 512 + dof:sb_o * 512 + dof + 128],
                          psA[:, sb_o * 512 + dof:sb_o * 512 + dof + 128], dloc[:], inc="sV")
                    p.wait("scalar", "sV")
                p.wait("scalar", "sT", t_sc)
                p.wait("scalar", "sV", t_negB)
                if kc >= 2:
                    p.wait("scalar", "sT", av_done[kc - 2])
                eb_e = expt[(kc % 2) * 2]
                eb_o = expt[(kc % 2) * 2 + 1]
                p.run("scalar", "activation", eb_e[:], psA[:, sb_e * 512:(sb_e + 1) * 512], AF.Exp,
                      bias=smc[:, 4:5], inc="sS")
                p.run("scalar", "activation", eb_o[:], psA[:, sb_o * 512:(sb_o + 1) * 512], AF.Exp,
                      bias=smc[:, 6:7], inc="sS")
                exp_tok[kc] = p.cnt["sS"]
                if stage == "p0b":
                    av_done[kc] = p.cnt["sT"]
                    continue
                p.wait("tensor", "sS", exp_tok[kc])
                p.run("tensor", "matmul", psA[0:65, 2048:2560],
                      vrec[:, kc * 1056 + he * 66:kc * 1056 + he * 66 + 65], eb_e[:],
                      start=(kc == 0), stop=(kc == 15))
                p.run("tensor", "matmul", psA[64:128, 2560:3072],
                      vrec[:, kc * 1056 + ho * 66:kc * 1056 + ho * 66 + 64], eb_o[:],
                      start=(kc == 0), stop=(kc == 15), tile_position=(0, 64))
                p.run("tensor", "matmul", psA[0:1, 3072:3584], ones1b[:], eb_o[:],
                      start=(kc == 0), stop=(kc == 15), inc="sT")
                av_done[kc] = p.cnt["sT"]
            t_av = p.cnt["sT"]
            if stage == "p0b":
                av_clear = p.cnt["sV"]
                continue
            p.wait("vector", "sT", t_av)
            p.wait("vector", "qS")
            p.run("vector", "reciprocal", rcp_t[64:65, :], psA[64:65, 2048:2560], inc="sV")
            p.run("vector", "reciprocal", rcpO[:], psA[0:1, 3072:3584], inc="sV")
            trc = p.cnt["sV"]
            p.wait("sync", "sV", trc)
            dma("sync", rcpE0[:], rcp_t[64:65, :])
            bc(rcpE0[0:1, :], rb_t[:, 0:512], 512)
            bc(rcpO[0:1, :], rb_t[:, 512:1024], 512)
            trb = p.cnt["qS"]
            p.wait("vector", "qS", trb)
            aosl = ao_sb[:, pair * S + qlo:pair * S + qlo + 512]
            p.run("vector", "tensor_mul", aosl[0:64, :], psA[0:64, 2048:2560],
                  rb_t[0:64, 0:512], inc="sV")
            p.run("vector", "tensor_mul", aosl[64:128, :], psA[64:128, 2560:3072],
                  rb_t[64:128, 512:1024], inc="sV")
            av_clear = p.cnt["sV"]
        if pair == 0:
            tap("exp00", expt[0][:], bf=True)
    t_heads = p.cnt["sV"]
    tap("ao", ao_sb[:], bf=True)
    if stage in ("pair0", "heads", "p0a", "p0b", "p0k", "p0n", "p0r"):
        return finalize()

    p.wait("sync", "sV", t_heads)
    p.wait("sync", "sT")
    dma("sync", xB_t[:].rearrange("p (c t) -> p c t", c=4), xsp[:, 4:8, :])
    dma("sync", x_sb[:, 0:4 * S].rearrange("p (c t) -> p c t", c=4), xsp[:, 0:4, :])
    t_xrel = p.cnt["qS"]

    def xattn(c):
        if c < 4:
            return x_sb[:, c * S:(c + 1) * S]
        return xB_t[:, (c - 4) * S:(c - 4 + 1) * S]
    wo_evac = {}
    for fo in range(NCH):
        for n in range(4):
            idx = fo * 4 + n
            bank = idx % 2
            if idx >= 2:
                p.wait("tensor", "sV", wo_evac[idx - 2])
            p.wait("tensor", "qG", t_wo)
            p.wait("tensor", "sV", t_heads)
            for fi in range(NCH):
                p.run("tensor", "matmul", psA[:, bank * 512:(bank + 1) * 512],
                      wslice(wo_sb, D, fi, fo), ao_sb[:, fi * S + n * 512:fi * S + (n + 1) * 512],
                      start=(fi == 0), stop=(fi == NCH - 1), inc="sT" if fi == NCH - 1 else None)
            p.wait("vector", "sT")
            p.wait("vector", "qS", t_xrel)
            p.run("vector", "tensor_add", xattn(fo)[:, n * 512:(n + 1) * 512],
                  psA[:, bank * 512:(bank + 1) * 512], xattn(fo)[:, n * 512:(n + 1) * 512],
                  inc="sV")
            wo_evac[idx] = p.cnt["sV"]
    t_x2 = p.cnt["sV"]
    if stage == "wo":
        p.wait("sync", "sV", t_x2)
        dma("sync", x_sb[:, 4 * S:8 * S], xB_t[:])
        return finalize()

    p.wait("sync", "sV", t_x2)
    dma("sync", x_sb[:, 4 * S:8 * S], xB_t[:])
    t_xmove = p.cnt["qS"]
    tap("x2", x_sb[:])

    p.wait("gpsimd", "sV", t_heads)
    t_cngc2 = load_cngc()

    t_xfin = emit_ffn(xc, [("qS", t_xmove), ("sV", t_x2)], t_cngc2)

    return finalize()


_NC_CACHE = {}


def _get_nc(taps=()):
    key = tuple(sorted(taps))
    if key not in _NC_CACHE:
        _NC_CACHE[key] = build_program(taps=key)
    return _NC_CACHE[key]


_WMAP = {
    "wq": "L0_attn_W_q", "wk": "L0_attn_W_k", "wv": "L0_attn_W_v", "wo": "L0_attn_W_o",
    "wdn": "L0_attn_W_down_kv", "wuk": "L0_attn_W_up_k", "wuv": "L0_attn_W_up_v",
    "cw1": "L0_ffn_cn_w1", "cw2": "L0_ffn_cn_w2", "cw3": "L0_ffn_cn_w3", "cw4": "L0_ffn_cn_w4",
    "gw1": "L0_ffn_gc_w1", "gw2": "L0_ffn_gc_w2", "gw3": "L0_ffn_gc_w3",
    "cb1": "L0_ffn_cn_b1", "cb2": "L0_ffn_cn_b2", "cb3": "L0_ffn_cn_b3",
    "gb1": "L0_ffn_gc_b1", "gb2": "L0_ffn_gc_b2",
}
_SMAP = {"cb4": "L0_ffn_cn_b4", "gb3": "L0_ffn_gc_b3", "ar": "L0_ffn_adaptive_range"}


def _make_in_maps(inputs):
    x = np.asarray(inputs["L0_attn_x"])
    shared = {k: np.ascontiguousarray(np.asarray(inputs[v]), dtype=np.float32)
              for k, v in _WMAP.items()}
    for k, v in _SMAP.items():
        shared[k] = np.ascontiguousarray(np.asarray(inputs[v]), dtype=np.float32).reshape(1, 1)
    in_maps = []
    for b in range(8):
        m = dict(shared)
        m["x"] = np.ascontiguousarray(x[b].T.astype(np.float32))
        in_maps.append(m)
    return in_maps


def kernel(**inputs):
    nc = _get_nc()
    in_maps = _make_in_maps(inputs)
    res = run_bass_kernel_spmd(nc, in_maps, core_ids=list(range(8)))
    out = np.stack([np.asarray(res.results[b]["y"]).T for b in range(8)], axis=0)
    return np.ascontiguousarray(out.astype(np.float32))


# version 02
# speedup vs baseline: 1.2600x; 1.2600x over previous
import math
import numpy as np
import concourse.bass as bass
import concourse.mybir as mybir
from concourse.bass_utils import run_bass_kernel_spmd

F32 = mybir.dt.float32
BF16 = mybir.dt.bfloat16
AF = mybir.ActivationFunctionType
ALU = mybir.AluOpType
AX = mybir.AxisListType.X

D = 1024
S = 2048
NCH = 8
H = 16
HD = 64
DL = 256
LAM = 0.01
EPS = 1e-5
QT = 512
NQ = S // QT

ENGINES = ("tensor", "vector", "scalar", "sync", "gpsimd")


class Prog:

    def __init__(self, nc):
        self.nc = nc
        self.q = {e: [] for e in ENGINES}
        self.handles = {}
        self.cnt = {}
        self.waited = {e: {} for e in ENGINES}
        self.dirty = {e: set() for e in ENGINES}
        self.reading = {e: set() for e in ENGINES}

    def add_sem(self, name, handle):
        self.handles[name] = handle
        self.cnt[name] = 0

    TRACKED = {"vector": "sV", "scalar": "sS"}

    @staticmethod
    def _ap_name(a):
        t = getattr(a, "tensor", None)
        if t is None:
            return None
        return getattr(t, "name", None) or str(t)

    def wait(self, eng, name, val=None, force=False):
        if val is None:
            val = self.cnt[name]
        if val <= 0:
            return
        assert val <= self.cnt[name], (eng, name, val, self.cnt[name])
        if not force and self.waited[eng].get(name, 0) >= val:
            return
        self.waited[eng][name] = max(self.waited[eng].get(name, 0), val)
        self.q[eng].append(("wait", name, val))
        if eng in self.TRACKED and name == self.TRACKED[eng] and val == self.cnt[name]:
            self.dirty[eng] = set()
            self.reading[eng] = set()

    def run(self, eng, method, *args, inc=None, incn=1, **kwargs):
        own = self.TRACKED.get(eng)
        if own is not None and method != "dma_start":
            if inc is None:
                inc = own
            names = []
            for a in list(args) + list(kwargs.values()):
                n = self._ap_name(a)
                if n is not None:
                    names.append(n)
            wname = names[0] if names else None
            rnames = set(names[1:])
            hazard = bool(rnames & self.dirty[eng]) or (
                wname is not None and (wname in self.dirty[eng] or wname in self.reading[eng]))
            if hazard and self.cnt[own] > 0:
                v = self.cnt[own]
                self.q[eng].append(("wait", own, v))
                self.waited[eng][own] = max(self.waited[eng].get(own, 0), v)
                self.dirty[eng] = set()
                self.reading[eng] = set()
            if wname is not None:
                self.dirty[eng].add(wname)
            self.reading[eng].update(rnames)
        tok = None
        if inc is not None:
            self.cnt[inc] += incn
            tok = self.cnt[inc]
            if self.waited[eng].get(inc, 0) < tok:
                self.waited[eng][inc] = tok
        self.q[eng].append(("call", method, args, kwargs, (inc, incn) if inc else None))
        return tok

    def replay(self, block):
        def mk(ename):
            def body(eng):
                for item in self.q[ename]:
                    if item[0] == "wait":
                        eng.wait_ge(self.handles[item[1]], item[2])
                    else:
                        _, m, args, kwargs, incinfo = item
                        ins = getattr(eng, m)(*args, **kwargs)
                        if incinfo is not None:
                            ins.then_inc(self.handles[incinfo[0]], incinfo[1])
            return body
        block.tensor(mk("tensor"))
        block.vector(mk("vector"))
        block.scalar(mk("scalar"))
        block.sync(mk("sync"))
        block.gpsimd(mk("gpsimd"))


def bcast_ap(row_ap, npart, width):
    from concourse.ap import AP
    return AP(row_ap.tensor, row_ap.offset, [[1, 1], [0, npart], [1, width]])


def build_program(taps=(), stage="full"):
    nc = bass.Bass("TRN2", target_bir_lowering=False, debug=False)
    taps = set(taps)

    xin = nc.dram_tensor("x", [D, S], F32, kind="ExternalInput")
    wshapes = {
        "wq": [D, D], "wk": [D, D], "wv": [D, D], "wo": [D, D],
        "wdn": [D, DL], "wuk": [DL, D], "wuv": [DL, D],
        "cw1": [D, D], "cw2": [D, 512], "cw3": [512, 256], "cw4": [256, 1],
        "gw1": [D, 512], "gw2": [512, 256], "gw3": [256, 1],
    }
    W = {k: nc.dram_tensor(k, v, F32, kind="ExternalInput") for k, v in wshapes.items()}
    bshapes = {"cb1": [D], "cb2": [512], "cb3": [256], "gb1": [512], "gb2": [256]}
    Bv = {k: nc.dram_tensor(k, v, F32, kind="ExternalInput") for k, v in bshapes.items()}
    Bs = {k: nc.dram_tensor(k, [1, 1], F32, kind="ExternalInput") for k in ("cb4", "gb3", "ar")}
    yout = nc.dram_tensor("y", [D, S], F32, kind="ExternalOutput")
    xspill = nc.dram_tensor("xspill", [D, S], F32, kind="Internal")

    tap_t = {}
    tshapes = {"x1": [128, NCH * S], "xln2": [128, NCH * S], "kv": [128, NCH * S],
               "latent": [128, 2 * S], "q": [128, NCH * S], "vrec": [128, 16 * 1056],
               "ao": [128, NCH * S], "x2": [128, NCH * S], "gf": [1, 16],
               "ab": [128, S], "bb": [128, S], "exp00": [128, 512], "nb": [2, 8],
               "S1": [128, NCH * QT], "gacc": [128, 8], "g1b": [128, 8], "g2b": [128, 4], "gaccb": [128, 16], "gpre": [128, 4]}
    for t in taps:
        tap_t[t] = nc.dram_tensor("tap_" + t, tshapes[t], F32, kind="ExternalOutput")

    tpos = np.arange(S, dtype=np.float64)
    u_neg = (np.exp(-LAM * tpos) / math.sqrt(32.0)).astype(np.float32)
    u_pos = (np.exp(+LAM * tpos) / math.sqrt(32.0)).astype(np.float32)
    UNEG_h = nc.inline_tensor(np.ascontiguousarray(np.tile(u_neg, (128, 1))), "c_uneg")
    UPOS_h = nc.inline_tensor(np.ascontiguousarray(np.tile(u_pos, (128, 1))), "c_upos")
    ii = np.arange(128, dtype=np.float64)
    DLOC_h = nc.inline_tensor(
        (np.exp(-LAM * np.abs(ii[:, None] - ii[None, :])) / 32.0).astype(np.float32), "c_dloc")
    ones2_np = np.zeros((128, 2), dtype=np.float32)
    ones2_np[0:64, 0] = 1.0
    ones2_np[64:128, 1] = 1.0
    ONES2_h = nc.inline_tensor(ones2_np, "c_ones2")

    A_X = 0
    A_RB = 65536
    A_CG = A_RB + 61440
    A_SM = A_CG + 38912
    arena_len = 204800

    ph = nc.alloc_sbuf_tensor("arena", [128, arena_len], mybir.dt.uint8)
    base = nc.lookup_mloc(ph).addr
    assert base + arena_len <= nc.sbuf_top, (base, arena_len, nc.sbuf_top)

    def at(name, off, shape, dtype):
        nbytes = int(np.prod(shape[1:])) * (2 if dtype == BF16 else 4)
        assert off + nbytes <= arena_len, (name, off, nbytes)
        return nc.alloc_sbuf_tensor_at(name, shape, dtype, offset=base + off)

    RB, CG, SM = A_RB, A_CG, A_SM
    x_sb = at("x_sb", A_X, [128, NCH * S], F32)
    uneg = at("uneg", SM + 0, [128, S], F32)
    upos = at("upos", SM + 8192, [128, S], F32)
    dloc = at("dloc", SM + 16384, [128, 128], F32)
    ones1b = at("ones1b", SM + 16896, [128, 1], BF16)
    ones2b = at("ones2b", SM + 16960, [128, 4], BF16)
    rowA = at("rowA", SM + 17024, [1, 1024], F32)
    rowB = at("rowB", SM + 21120, [1, 1024], F32)
    cb1 = at("cb1s", SM + 25216, [128, 8], F32)
    cb2 = at("cb2s", SM + 25248, [128, 4], F32)
    cb3 = at("cb3s", SM + 25280, [128, 2], F32)
    gb1 = at("gb1s", SM + 25312, [128, 4], F32)
    gb2 = at("gb2s", SM + 25344, [128, 2], F32)
    sm1 = at("sm1", SM + 25376, [1, 16], F32)
    smc = at("smc", SM + 25440, [128, 8], F32)
    gacc = at("gacc", SM + 25472, [128, 8], F32)
    gaccb = at("gaccb", SM + 25504, [128, 16], BF16)
    g1b = at("g1b", SM + 25536, [128, 8], BF16)
    g2b = at("g2b", SM + 25568, [128, 4], BF16)
    kn2 = at("kn2", SM + 25600, [2, 8], F32)
    epsc = at("epsc", SM + 25664, [1, 1], F32)
    onesF = at("onesF", SM + 25696, [1, 128], F32)
    rcpE0 = at("rcpE0", SM + 26240, [1, 512], F32)
    rcpO = at("rcpO", SM + 28288, [1, 512], F32)
    wuk_sb = at("wuk_sb", SM + 32384, [128, 2 * D], BF16)
    rcp_t = at("rcp_t", SM + 30336, [128, 512], F32)
    SM_CB4, SM_GB3, SM_AR, SM_GF, SM_A, SM_B, SM_T1, _u7, SM_QMAX, SM_LNQ, _u10, SM_T2 = range(12)
    SM_BSUM, SM_C = 12, 14

    ab_t = at("ab", RB + 0, [128, S], F32)
    bb_t = at("bb", RB + 8192, [128, S], F32)
    S_t = at("S_st", RB + 16384, [128, NCH * QT], F32)
    Sb_t = at("Sb_st", RB + 32768, [128, NCH * QT], BF16)
    h1b = at("h1b", RB + 40960, [128, NCH * QT], BF16)
    h2b = at("h2b", RB + 49152, [128, 4 * QT], BF16)
    h3b = at("h3b", RB + 53248, [128, 2 * QT], BF16)
    cbt = at("cbt", RB + 55296, [128, QT], F32)
    tmpA = at("tmpA", RB + 57344, [128, QT], F32)
    tmpB = at("tmpB", RB + 59392, [128, QT], F32)
    xbp = [at("xbp0", RB + 32768, [128, S], BF16),
           at("xbp1", RB + 36864, [128, S], BF16)]
    sqp = [at("sqp0", RB + 49152, [128, S], BF16),
           at("sqp1", RB + 53248, [128, S], BF16)]
    gsc = at("gsc", RB + 40960, [128, S], F32)

    xln2b = at("xln2b", RB + 16384, [128, NCH * S], BF16)
    vrec = at("vrec", RB + 0, [128, 16 * 1056], BF16)
    wo_sb = at("wo_sb", RB + 34304, [128, NCH * D], BF16)
    xB_t = at("xB", RB + 0, [128, 4 * S], F32)

    kv_sb = at("kv_sb", A_X + 0, [128, NCH * S], BF16)
    q_sb = at("q_sb", A_X + 0, [128, NCH * S], BF16)
    ao_sb = at("ao_sb", A_X + 32768, [128, NCH * S], BF16)
    wk_sb = at("wk_sb", A_X + 32768, [128, NCH * D], BF16)
    wv_sb = at("wv_sb", A_X + 49152, [128, NCH * D], BF16)
    wq_sb = at("wq_sb", A_X + 32768, [128, NCH * D], BF16)
    wdn_sb = at("wdn_sb", A_X + 49152, [128, 8 * 256], BF16)
    wuv_sb = at("wuv_sb", A_X + 53248, [128, 2 * D], BF16)
    xst = at("xst", CG + 0, [128, S], F32)

    cw1 = at("cw1s", CG + 0, [128, 8 * D], BF16)
    cw2 = at("cw2s", CG + 16384, [128, 8 * 512], BF16)
    cw3 = at("cw3s", CG + 24576, [128, 4 * 256], BF16)
    cw4 = at("cw4s", CG + 26624, [128, 4], BF16)
    gw1 = at("gw1s", CG + 26688, [128, 8 * 512], BF16)
    gw2 = at("gw2s", CG + 34880, [128, 4 * 256], BF16)
    gw3 = at("gw3s", CG + 36928, [128, 4], BF16)

    latent = at("latent", CG + 0, [128, 2 * S], BF16)
    ktil = at("ktil", CG + 8192, [128, S], BF16)
    khat = at("khat", CG + 12288, [128, S], BF16)
    qtil = at("qtil", CG + 16384, [128, S], BF16)
    qhat = at("qhat", CG + 20480, [128, S], BF16)
    kraw = at("kraw", CG + 24576, [128, S], BF16)
    expt = [at(f"expt{i}", CG + 28672 + i * 1024, [128, 512], BF16) for i in range(4)]
    rb_t = at("rb_t", CG + 32768, [128, 1024], F32)
    ksq_t = at("ksq", CG + 36864, [128, 512], BF16)
    qsq_t = [at("qsq0", CG + 8192, [128, 512], BF16),
             at("qsq1", CG + 12288, [128, 512], BF16)]

    psA = nc.alloc_psum_tensor("psA", [128, 4096], F32)

    from contextlib import ExitStack
    ctx = ExitStack()
    p = Prog(nc)
    for sname in ("sT", "sV", "sS", "qS", "qG"):
        p.add_sem(sname, ctx.enter_context(nc.semaphore(sname)))

    def dma(eng_q, out, in_, slow=False):
        sem = "qS" if eng_q == "sync" else "qG"
        kw = {"allow_slow_non_contiguous": True} if slow else {}
        return p.run(eng_q, "dma_start", inc=sem, incn=16, out=out, in_=in_, **kw)

    def bc(row_ap, dst_ap, width, npart=128):
        return dma("sync", dst_ap, bcast_ap(row_ap, npart, width))

    def bc_scalar(cell_ap, dst_col, pscol):
        p.wait("tensor", "sV")
        p.run("tensor", "matmul", psA[:, 3584 + pscol:3584 + pscol + 1], onesF[:], cell_ap,
              start=True, stop=True, inc="sT")
        p.wait("vector", "sT")
        p.run("vector", "tensor_copy", dst_col, psA[:, 3584 + pscol:3584 + pscol + 1], inc="sV")
        return p.cnt["sV"]

    def xc(c):
        return x_sb[:, c * S:(c + 1) * S]

    def load_weight(dst, hbm, rows, slow=False):
        nchk = rows // 128
        src = hbm.ap().rearrange("(c p) o -> p c o", p=128)
        return dma("gpsimd", dst[:].rearrange("p (c o) -> p c o", c=nchk), src, slow=slow)

    def col2(tile, i):
        return tile[:, 2 * i:2 * i + 1]

    def wslice(wsb, ow, fi, fo, w=128):
        return wsb[:, fi * ow + fo * w:fi * ow + fo * w + w]

    def tap(name, src_ap, bf=False):
        if name not in tap_t:
            return
        q = "gpsimd" if bf else "sync"
        for sname2 in ("sT", "sV", "sS"):
            p.wait(q, sname2)
        dma(q, tap_t[name].ap(), src_ap)

    dma("sync", x_sb[:].rearrange("p (c t) -> p c t", c=NCH),
        xin.ap().rearrange("(c p) t -> p c t", p=128))
    t_x_loaded = p.cnt["qS"]
    dma("sync", uneg[:], UNEG_h.ap())
    dma("sync", upos[:], UPOS_h.ap())
    dma("sync", dloc[:], DLOC_h.ap())
    dma("gpsimd", ones2b[:].rearrange("p (c o) -> p c o", o=2)[:, :, 0:1], ONES2_h.ap().rearrange("p (c o) -> p c o", o=1), slow=True)
    p.run("vector", "memset", ones1b[:], 1.0, inc="sV")
    p.run("vector", "memset", epsc[:], EPS, inc="sV")
    p.run("vector", "memset", onesF[:], 1.0, inc="sV")
    dma("sync", cb1[:], Bv["cb1"].ap().rearrange("(c p) -> p c", p=128), slow=True)
    dma("sync", cb2[:], Bv["cb2"].ap().rearrange("(c p) -> p c", p=128), slow=True)
    dma("sync", cb3[:], Bv["cb3"].ap().rearrange("(c p) -> p c", p=128), slow=True)
    dma("sync", gb1[:], Bv["gb1"].ap().rearrange("(c p) -> p c", p=128), slow=True)
    dma("sync", gb2[:], Bv["gb2"].ap().rearrange("(c p) -> p c", p=128), slow=True)
    dma("sync", sm1[0:1, SM_CB4:SM_CB4 + 1], Bs["cb4"].ap())
    dma("sync", sm1[0:1, SM_GB3:SM_GB3 + 1], Bs["gb3"].ap())
    dma("sync", sm1[0:1, SM_AR:SM_AR + 1], Bs["ar"].ap())
    t_bias = p.cnt["qS"]

    def load_cngc():
        load_weight(cw1, W["cw1"], D)
        load_weight(cw2, W["cw2"], D)
        load_weight(cw3, W["cw3"], 512)
        dma("gpsimd", cw4[:].rearrange("p (c o) -> p c o", o=2)[:, :, 0:1], W["cw4"].ap().rearrange("(c p) o -> p c o", p=128), slow=True)
        load_weight(gw1, W["gw1"], D)
        load_weight(gw2, W["gw2"], 512)
        dma("gpsimd", gw3[:].rearrange("p (c o) -> p c o", o=2)[:, :, 0:1], W["gw3"].ap().rearrange("(c p) o -> p c o", p=128), slow=True)
        return p.cnt["qG"]
    t_cngc1 = load_cngc()

    def emit_stats(xap, wait_toks):
        stat_tok = {}
        for c in range(NCH):
            xb = xbp[c % 2]
            sq = sqp[c % 2]
            for sem, val in wait_toks:
                p.wait("vector", sem, val)
                p.wait("scalar", sem, val)
                p.wait("tensor", sem, val)
            if c >= 2:
                p.wait("vector", "sT", stat_tok[c - 2])
                p.wait("scalar", "sT", stat_tok[c - 2])
            p.run("vector", "tensor_copy", xb[:], xap(c), inc="sV")
            tv = p.cnt["sV"]
            p.run("scalar", "activation", sq[:], xap(c), AF.Square, inc="sS")
            ts = p.cnt["sS"]
            p.wait("tensor", "sV", tv)
            p.wait("tensor", "sS", ts)
            for j in range(4):
                p.run("tensor", "matmul", psA[0:1, j * 512:(j + 1) * 512], ones1b[:],
                      xb[:, j * 512:(j + 1) * 512], start=(c == 0), stop=(c == NCH - 1))
                p.run("tensor", "matmul", psA[0:1, 2048 + j * 512:2048 + (j + 1) * 512], ones1b[:],
                      sq[:, j * 512:(j + 1) * 512], start=(c == 0), stop=(c == NCH - 1),
                      inc="sT" if j == 3 else None)
            stat_tok[c] = p.cnt["sT"]
        return p.cnt["sT"]

    def emit_rows_ab(t_stats):
        for hh in range(2):
            hs = hh * 1024
            p.wait("vector", "sT", t_stats)
            p.wait("vector", "qS")
            p.run("vector", "tensor_scalar", rowA[:], psA[0:1, hs:hs + 1024], 1.0 / D, None,
                  op0=ALU.mult, inc="sV")
            p.wait("scalar", "sV")
            p.run("scalar", "activation", rowB[:], rowA[:], AF.Square, inc="sS")
            p.wait("vector", "sS")
            p.run("vector", "scalar_tensor_tensor", rowB[:], psA[0:1, 2048 + hs:2048 + hs + 1024],
                  1.0 / D, rowB[:], op0=ALU.mult, op1=ALU.subtract, inc="sV")
            p.wait("scalar", "sV")
            p.run("scalar", "activation", rowB[:], rowB[:], AF.Sqrt, bias=epsc[0:1, 0:1], inc="sS")
            p.wait("vector", "sS")
            p.run("vector", "reciprocal", rowA[0:1, 0:1024], rowB[:], inc="sV")
            p.run("vector", "tensor_copy", rowB[:], rowA[0:1, 0:1024], inc="sV")
            p.run("vector", "tensor_scalar", rowA[:], psA[0:1, hs:hs + 1024], 1.0 / D, None,
                  op0=ALU.mult, inc="sV")
            p.run("vector", "scalar_tensor_tensor", rowA[:], rowA[:], -1.0, rowB[:],
                  op0=ALU.mult, op1=ALU.mult, inc="sV")
            trow = p.cnt["sV"]
            p.wait("sync", "sV", trow)
            bc(rowB[0:1, :], ab_t[:, hs:hs + 1024], 1024)
            bc(rowA[0:1, :], bb_t[:, hs:hs + 1024], 1024)
        return p.cnt["qS"]

    def emit_ffn(xap, wait_toks, t_w):
        t_stats = emit_stats(xap, wait_toks)
        for sem, val in wait_toks:
            p.wait("sync", sem, val)
        t_ab = emit_rows_ab(t_stats)
        tap("ab", ab_t[:])
        tap("bb", bb_t[:])
        p.wait("vector", "qS", t_ab)
        for c in range(NCH):
            p.run("vector", "tensor_mul", gsc[:], xap(c), ab_t[:], inc="sV")
            p.run("vector", "reduce_sum", gacc[:, c:c + 1], gsc[:], AX, inc="sV")
        p.run("vector", "reduce_sum", sm1[0:1, SM_BSUM:SM_BSUM + 1], bb_t[0:1, :], AX, inc="sV")
        bc_scalar(sm1[0:1, SM_BSUM:SM_BSUM + 1], smc[:, 0:1], 0)
        p.run("vector", "tensor_scalar", gacc[:], gacc[:], smc[:, 0:1], 1.0 / S,
              op0=ALU.add, op1=ALU.mult, inc="sV")
        p.run("vector", "tensor_copy", gaccb[:].rearrange("p (c o) -> p c o", o=2)[:, :, 0:1], gacc[:].rearrange("p (c o) -> p c o", o=1), inc="sV")
        tg = p.cnt["sV"]
        tap("gacc", gacc[:])
        p.wait("tensor", "sV", tg)
        p.wait("tensor", "sS")
        p.wait("tensor", "qG", t_w)
        for fo in range(4):
            for fi in range(NCH):
                p.run("tensor", "matmul", psA[:, fo:fo + 1], wslice(gw1, 512, fi, fo),
                      col2(gaccb, fi), start=(fi == 0), stop=(fi == NCH - 1),
                      inc="sT" if fi == NCH - 1 else None)
        p.wait("scalar", "sT")
        p.wait("scalar", "qS", t_bias)
        if "gpre" in tap_t:
            p.wait("vector", "sT")
            p.run("vector", "tensor_copy", gacc[:, 0:4], psA[:, 0:4], inc="sV")
            tap("gpre", gacc[:, 0:4])
        for fo in range(4):
            p.run("scalar", "activation", col2(g1b, fo), psA[:, fo:fo + 1], AF.Gelu,
                  bias=gb1[:, fo:fo + 1], inc="sS")
        p.wait("tensor", "sS")
        for fo in range(2):
            for fi in range(4):
                p.run("tensor", "matmul", psA[:, 512 + fo:512 + fo + 1], wslice(gw2, 256, fi, fo),
                      col2(g1b, fi), start=(fi == 0), stop=(fi == 3),
                      inc="sT" if fi == 3 else None)
        p.wait("scalar", "sT")
        for fo in range(2):
            p.run("scalar", "activation", col2(g2b, fo), psA[:, 512 + fo:512 + fo + 1], AF.Gelu,
                  bias=gb2[:, fo:fo + 1], inc="sS")
        p.wait("tensor", "sS")
        for fi in range(2):
            p.run("tensor", "matmul", psA[0:1, 1024:1025], col2(gw3, fi), col2(g2b, fi),
                  start=(fi == 0), stop=(fi == 1), inc="sT" if fi == 1 else None)
        p.wait("scalar", "sT")
        p.run("scalar", "activation", sm1[0:1, SM_GF:SM_GF + 1], psA[0:1, 1024:1025], AF.Sigmoid,
              bias=sm1[0:1, SM_GB3:SM_GB3 + 1], inc="sS")
        p.wait("vector", "sS")
        ar_ap = sm1[0:1, SM_AR:SM_AR + 1]
        p.run("vector", "tensor_scalar", sm1[0:1, SM_T2:SM_T2 + 1], sm1[0:1, SM_GF:SM_GF + 1],
              ar_ap, None, op0=ALU.mult, inc="sV")
        p.run("vector", "tensor_scalar", sm1[0:1, SM_T1:SM_T1 + 1], ar_ap,
              -1.0, 0.1, op0=ALU.mult, op1=ALU.add, inc="sV")
        p.run("vector", "tensor_scalar", sm1[0:1, SM_A:SM_A + 1], sm1[0:1, SM_T2:SM_T2 + 1],
              1.4, sm1[0:1, SM_T1:SM_T1 + 1], op0=ALU.mult, op1=ALU.add, inc="sV")
        p.run("vector", "tensor_scalar", sm1[0:1, SM_B:SM_B + 1], ar_ap,
              0.6, None, op0=ALU.mult, inc="sV")
        t_AB = p.cnt["sV"]
        tap("gf", sm1[:])
        tap("gaccb", gaccb[:], bf=True)
        tap("g1b", g1b[:], bf=True)
        tap("g2b", g2b[:], bf=True)

        SIGN = {0: (+1, -1), 1: (+1, -1), 2: (-1, +1), 3: (-1, +1)}
        t_S = p.cnt["sV"]
        prev_cb = 0
        for qq in range(NQ):
            qs = qq * QT
            p.wait("vector", "sT")
            p.wait("vector", "qS", t_ab)
            for c in range(NCH):
                xsl = xap(c)[:, qs:qs + QT]
                Ssl = S_t[:, c * QT:(c + 1) * QT]
                p.run("vector", "tensor_mul", Ssl, xsl, ab_t[:, qs:qs + QT], inc="sV")
                p.run("vector", "tensor_add", Ssl, Ssl, bb_t[:, qs:qs + QT], inc="sV")
                p.run("vector", "tensor_copy", Sb_t[:, c * QT:(c + 1) * QT], Ssl, inc="sV")
            t_S = p.cnt["sV"]
            if qq == 0:
                tap("S1", S_t[:])
            for cyc in range(3):
                p.wait("tensor", "sV", t_S)
                p.wait("tensor", "sS")
                h_evac = {}
                for fo in range(NCH):
                    bank = fo % 2
                    if fo >= 2:
                        p.wait("tensor", "sS", h_evac[fo - 2])
                    for fi in range(NCH):
                        p.run("tensor", "matmul", psA[:, bank * 512:(bank + 1) * 512],
                              wslice(cw1, D, fi, fo), Sb_t[:, fi * QT:(fi + 1) * QT],
                              start=(fi == 0), stop=(fi == NCH - 1),
                              inc="sT" if fi == NCH - 1 else None)
                    p.wait("scalar", "sT")
                    p.run("scalar", "activation", h1b[:, fo * QT:(fo + 1) * QT],
                          psA[:, bank * 512:(bank + 1) * 512], AF.Gelu,
                          bias=cb1[:, fo:fo + 1], inc="sS")
                    h_evac[fo] = p.cnt["sS"]
                for fo in range(4):
                    bank = 2 + fo % 2
                    if fo >= 2:
                        p.wait("tensor", "sS", h_evac[NCH + fo - 2])
                    for fi in range(NCH):
                        p.wait("tensor", "sS", h_evac[fi])
                        p.run("tensor", "matmul", psA[:, bank * 512:(bank + 1) * 512],
                              wslice(cw2, 512, fi, fo), h1b[:, fi * QT:(fi + 1) * QT],
                              start=(fi == 0), stop=(fi == NCH - 1),
                              inc="sT" if fi == NCH - 1 else None)
                    p.wait("scalar", "sT")
                    p.run("scalar", "activation", h2b[:, fo * QT:(fo + 1) * QT],
                          psA[:, bank * 512:(bank + 1) * 512], AF.Gelu,
                          bias=cb2[:, fo:fo + 1], inc="sS")
                    h_evac[NCH + fo] = p.cnt["sS"]
                for fo in range(2):
                    bank = 4 + fo
                    for fi in range(4):
                        p.wait("tensor", "sS", h_evac[NCH + fi])
                        p.run("tensor", "matmul", psA[:, bank * 512:(bank + 1) * 512],
                              wslice(cw3, 256, fi, fo), h2b[:, fi * QT:(fi + 1) * QT],
                              start=(fi == 0), stop=(fi == 3),
                              inc="sT" if fi == 3 else None)
                    p.wait("scalar", "sT")
                    p.run("scalar", "activation", h3b[:, fo * QT:(fo + 1) * QT],
                          psA[:, bank * 512:(bank + 1) * 512], AF.Gelu,
                          bias=cb3[:, fo:fo + 1], inc="sS")
                    h_evac[12 + fo] = p.cnt["sS"]
                for fi in range(2):
                    p.wait("tensor", "sS", h_evac[12 + fi])
                    p.run("tensor", "matmul", psA[0:1, 3072:3584], col2(cw4, fi),
                          h3b[:, fi * QT:(fi + 1) * QT], start=(fi == 0), stop=(fi == 1),
                          inc="sT" if fi == 1 else None)
                tz = p.cnt["sT"]
                p.wait("scalar", "sT", tz)
                p.wait("scalar", "sV")
                p.run("scalar", "activation", rowA[0:1, 0:512], psA[0:1, 3072:3584], AF.Sigmoid,
                      inc="sS")
                p.wait("vector", "sS")
                p.wait("vector", "sV", t_AB)
                p.wait("vector", "qS", prev_cb)
                p.run("vector", "tensor_scalar", rowB[0:1, 0:512], rowA[0:1, 0:512],
                      sm1[0:1, SM_B:SM_B + 1], sm1[0:1, SM_A:SM_A + 1],
                      op0=ALU.mult, op1=ALU.add, inc="sV")

                tc_row = p.cnt["sV"]
                p.wait("sync", "sV", tc_row)
                prev_cb = bc(rowB[0:1, 0:512], cbt[:], 512)
                p.wait("vector", "qS", prev_cb)
                for a in range(4):
                    b = a + 4
                    sa, sb = SIGN[a]
                    Sa = S_t[:, a * QT:(a + 1) * QT]
                    Sb_ = S_t[:, b * QT:(b + 1) * QT]
                    p.run("vector", "tensor_mul", tmpA[:], Sb_, cbt[:], inc="sV")
                    p.run("vector", "tensor_mul", tmpB[:], Sa, cbt[:], inc="sV")
                    p.run("vector", "tensor_tensor", Sa, Sa, tmpA[:],
                          op=(ALU.add if sa > 0 else ALU.subtract), inc="sV")
                    p.run("vector", "tensor_tensor", Sb_, Sb_, tmpB[:],
                          op=(ALU.add if sb > 0 else ALU.subtract), inc="sV")
                    if cyc < 2:
                        p.run("vector", "tensor_copy", Sb_t[:, a * QT:(a + 1) * QT], Sa, inc="sV")
                        p.run("vector", "tensor_copy", Sb_t[:, b * QT:(b + 1) * QT], Sb_, inc="sV")
                t_S = p.cnt["sV"]
            for c in range(NCH):
                p.run("vector", "scalar_tensor_tensor", xap(c)[:, qs:qs + QT],
                      S_t[:, c * QT:(c + 1) * QT], 0.5, xap(c)[:, qs:qs + QT],
                      op0=ALU.mult, op1=ALU.add, inc="sV")
        return p.cnt["sV"]

    def finalize():
        p.wait("sync", "sV")
        p.wait("sync", "sS")
        p.wait("sync", "sT")
        dma("sync", yout.ap().rearrange("(c p) t -> p c t", p=128),
            x_sb[:].rearrange("p (c t) -> p c t", c=NCH))
        p.wait("sync", "qS")
        p.wait("sync", "qG")
        with nc.Block() as block:
            p.replay(block)
        ctx.close()
        return nc

    if stage == "load":
        return finalize()

    t_x1 = emit_ffn(xc, [("qS", t_x_loaded)], t_cngc1)
    tap("x1", x_sb[:])
    if stage == "ffn1":
        return finalize()

    t2_stats = emit_stats(xc, [("sV", t_x1)])
    t2_ab = emit_rows_ab(t2_stats)
    p.wait("vector", "qS", t2_ab)
    for c in range(NCH):
        p.run("vector", "tensor_mul", xst[:], xc(c), ab_t[:], inc="sV")
        p.run("vector", "tensor_add", xln2b[:, c * S:(c + 1) * S], xst[:], bb_t[:], inc="sV")
    t_xln2 = p.cnt["sV"]
    tap("xln2", xln2b[:], bf=True)
    if stage == "xln2":
        return finalize()
    p.wait("sync", "sV", t_xln2)
    xsp = xspill.ap().rearrange("(c p) t -> p c t", p=128)
    dma("sync", xsp[:, 4:8, :], x_sb[:, 4 * S:].rearrange("p (c t) -> p c t", c=4))
    t_spill_hi = p.cnt["qS"]
    dma("sync", xsp[:, 0:4, :], x_sb[:, 0:4 * S].rearrange("p (c t) -> p c t", c=4))
    t_spill_lo = p.cnt["qS"]

    p.wait("gpsimd", "qS", t_spill_hi)
    load_weight(wk_sb, W["wk"], D)
    load_weight(wv_sb, W["wv"], D)
    t_wkv = p.cnt["qG"]

    kv_evac = {}
    for fo in range(NCH):
        for n in range(4):
            idx = fo * 4 + n
            bank = idx % 2
            if idx >= 2:
                p.wait("tensor", "sS", kv_evac[idx - 2])
            p.wait("tensor", "sV", t_xln2)
            p.wait("tensor", "qG", t_wkv)
            p.wait("tensor", "qS", t_spill_lo)
            for fi in range(NCH):
                p.run("tensor", "matmul", psA[:, bank * 512:(bank + 1) * 512],
                      wslice(wk_sb, D, fi, fo), xln2b[:, fi * S + n * 512:fi * S + (n + 1) * 512],
                      start=(fi == 0), stop=False)
            for fi in range(NCH):
                p.run("tensor", "matmul", psA[:, bank * 512:(bank + 1) * 512],
                      wslice(wv_sb, D, fi, fo), xln2b[:, fi * S + n * 512:fi * S + (n + 1) * 512],
                      start=False, stop=(fi == NCH - 1), inc="sT" if fi == NCH - 1 else None)
            p.wait("scalar", "sT")
            p.run("scalar", "activation", kv_sb[:, fo * S + n * 512:fo * S + (n + 1) * 512],
                  psA[:, bank * 512:(bank + 1) * 512], AF.Copy, inc="sS")
            kv_evac[idx] = p.cnt["sS"]
    t_kv = p.cnt["sS"]
    tap("kv", kv_sb[:], bf=True)
    if stage == "kv":
        return finalize()

    p.wait("gpsimd", "sT")
    load_weight(wdn_sb, W["wdn"], D)
    t_wdn = p.cnt["qG"]
    lat_evac = {}
    for fo in range(2):
        for n in range(4):
            idx = fo * 4 + n
            bank = idx % 2
            if idx >= 2:
                p.wait("tensor", "sS", lat_evac[idx - 2])
            p.wait("tensor", "sS", t_kv)
            p.wait("tensor", "qG", t_wdn)
            for fi in range(NCH):
                p.run("tensor", "matmul", psA[:, bank * 512:(bank + 1) * 512],
                      wslice(wdn_sb, DL, fi, fo), kv_sb[:, fi * S + n * 512:fi * S + (n + 1) * 512],
                      start=(fi == 0), stop=(fi == NCH - 1), inc="sT" if fi == NCH - 1 else None)
            p.wait("scalar", "sT")
            p.run("scalar", "activation", latent[:, fo * S + n * 512:fo * S + (n + 1) * 512],
                  psA[:, bank * 512:(bank + 1) * 512], AF.Copy, inc="sS")
            lat_evac[idx] = p.cnt["sS"]
    t_latent = p.cnt["sS"]
    tap("latent", latent[:], bf=True)
    if stage == "latent":
        return finalize()

    p.wait("gpsimd", "sT")
    load_weight(wq_sb, W["wq"], D)
    t_wq = p.cnt["qG"]
    q_evac, q_sq_s, q_sq_t = {}, {}, {}
    for fo in range(NCH):
        for n in range(4):
            idx = fo * 4 + n
            bank = idx % 2
            if idx >= 2:
                p.wait("tensor", "sV", q_evac[idx - 2])
            p.wait("tensor", "sS", t_latent)
            p.wait("tensor", "qG", t_wq)
            for fi in range(NCH):
                p.run("tensor", "matmul", psA[:, bank * 512:(bank + 1) * 512],
                      wslice(wq_sb, D, fi, fo), xln2b[:, fi * S + n * 512:fi * S + (n + 1) * 512],
                      start=(fi == 0), stop=(fi == NCH - 1), inc="sT" if fi == NCH - 1 else None)
            tmm = p.cnt["sT"]
            p.wait("vector", "sT", tmm)
            p.run("vector", "tensor_copy", q_sb[:, fo * S + n * 512:fo * S + (n + 1) * 512],
                  psA[:, bank * 512:(bank + 1) * 512], inc="sV")
            q_evac[idx] = p.cnt["sV"]
            if stage == "qnn":
                q_sq_s[idx] = p.cnt["sS"]
                q_sq_t[idx] = p.cnt["sT"]
                continue
            if idx >= 2:
                p.wait("vector", "sT", q_sq_t[idx - 2])
            qsq = qsq_t[bank]
            p.run("vector", "tensor_mul", qsq[:],
                  q_sb[:, fo * S + n * 512:fo * S + (n + 1) * 512],
                  q_sb[:, fo * S + n * 512:fo * S + (n + 1) * 512], inc="sV")
            q_sq_s[idx] = p.cnt["sV"]
            p.wait("tensor", "sV", q_sq_s[idx])
            p.run("tensor", "matmul", psA[0:1, 2048 + n * 512:2048 + (n + 1) * 512], ones1b[:],
                  qsq[:], start=(fo == 0), stop=(fo == NCH - 1), inc="sT")
            q_sq_t[idx] = p.cnt["sT"]
    t_q = p.cnt["sV"]
    t_qn = p.cnt["sT"]
    tap("q", q_sb[:], bf=True)
    if stage == "qnn":
        return finalize()
    if stage == "q":
        return finalize()
    p.wait("vector", "sT", t_qn)
    p.run("vector", "reduce_max", sm1[0:1, SM_QMAX:SM_QMAX + 1], psA[0:1, 2048:4096], AX, inc="sV")
    p.wait("scalar", "sV")
    p.run("scalar", "activation", sm1[0:1, SM_LNQ:SM_LNQ + 1], sm1[0:1, SM_QMAX:SM_QMAX + 1],
          AF.Ln, inc="sS")
    p.wait("scalar", "sS", force=True)
    p.wait("vector", "sS")
    p.run("vector", "tensor_scalar", sm1[0:1, SM_C:SM_C + 1], sm1[0:1, SM_LNQ:SM_LNQ + 1],
          0.5, -0.5 * math.log(32.0), op0=ALU.mult, op1=ALU.add, inc="sV")
    t_C = p.cnt["sV"]

    p.wait("gpsimd", "sT")
    load_weight(wuv_sb, W["wuv"], DL)
    t_wuv = p.cnt["qG"]
    p.wait("vector", "sV", t_q)
    p.wait("vector", "sT", t_qn)
    vr_view = vrec[:].rearrange("p (a w) -> p a w", w=66)
    p.run("vector", "memset", vr_view[:, :, 64:65], 1.0, inc="sV")
    vr_evac = {}
    for tc in range(16):
        for fh in range(2):
            idx = tc * 2 + fh
            bank = idx % 2
            if idx >= 2:
                p.wait("tensor", "sV", vr_evac[idx - 2])
            p.wait("tensor", "sS", t_latent)
            p.wait("tensor", "sV", t_q)
            p.wait("tensor", "sT", t_qn)
            p.wait("tensor", "qG", t_wuv)
            for d in range(2):
                p.run("tensor", "matmul", psA[:, bank * 512:(bank + 1) * 512],
                      latent[:, d * S + tc * 128:d * S + (tc + 1) * 128],
                      wuv_sb[:, d * D + fh * 512:d * D + (fh + 1) * 512],
                      start=(d == 0), stop=(d == 1), inc="sT" if d == 1 else None)
            p.wait("vector", "sT")
            dst = vrec[:, tc * 1056 + fh * 528:tc * 1056 + (fh + 1) * 528].rearrange(
                "p (h w) -> p h w", w=66)[:, :, 0:64]
            p.run("vector", "tensor_copy", dst,
                  psA[:, bank * 512:(bank + 1) * 512].rearrange("p (h w) -> p h w", w=64),
                  inc="sV")
            vr_evac[idx] = p.cnt["sV"]
    t_vrec = p.cnt["sV"]
    tap("vrec", vrec[:], bf=True)
    if stage == "vrec":
        return finalize()

    p.wait("gpsimd", "sV", t_q)
    load_weight(wuk_sb, W["wuk"], DL)
    t_wuk = p.cnt["qG"]
    load_weight(wo_sb, W["wo"], D)
    t_wo = p.cnt["qG"]

    av_clear = 0
    npairs = 1 if stage.startswith("p0") or stage == "pair0" else 8
    for pair in range(npairs):
        he, ho = 2 * pair, 2 * pair + 1
        p.wait("tensor", "qG", t_wuk)
        p.wait("tensor", "sS", t_latent)
        p.wait("tensor", "sV", t_vrec)
        p.wait("tensor", "sS")
        p.wait("tensor", "sV", av_clear)
        for n in range(4):
            for d in range(2):
                p.run("tensor", "matmul", psA[:, n * 512:(n + 1) * 512],
                      wuk_sb[:, d * D + pair * 128:d * D + (pair + 1) * 128],
                      latent[:, d * S + n * 512:d * S + (n + 1) * 512],
                      start=(d == 0), stop=(d == 1), inc="sT" if d == 1 else None)
        t_krec = p.cnt["sT"]
        p.wait("vector", "sT", t_krec)
        p.run("vector", "tensor_mul", ktil[:], psA[:, 0:2048], upos[:], inc="sV")
        p.run("vector", "tensor_mul", khat[:], psA[:, 0:2048], uneg[:], inc="sV")
        p.run("vector", "tensor_copy", kraw[:], psA[:, 0:2048], inc="sV")
        t_ktiles = p.cnt["sV"]
        if stage == "p0k":
            break
        for n in range(4):
            p.wait("vector", "sT")
            p.run("vector", "tensor_mul", ksq_t[:], ktil[:, n * 512:(n + 1) * 512],
                  khat[:, n * 512:(n + 1) * 512], inc="sV")
            tk = p.cnt["sV"]
            p.wait("tensor", "sV", tk)
            p.run("tensor", "matmul", psA[0:1, 2048 + n * 512:2048 + (n + 1) * 512],
                  col2(ones2b, 0), ksq_t[:], start=True, stop=True)
            p.run("tensor", "matmul", psA[0:1, n * 512:(n + 1) * 512],
                  col2(ones2b, 1), ksq_t[:], start=True, stop=True, inc="sT")
        t_kn = p.cnt["sT"]
        if stage == "p0n":
            break
        p.wait("vector", "sT", t_kn)
        p.run("vector", "reduce_max", kn2[0:1, 0:1], psA[0:1, 2048:4096], AX, inc="sV")
        p.run("vector", "reduce_max", kn2[0:1, 1:2], psA[0:1, 0:2048], AX, inc="sV")
        p.wait("scalar", "sV")
        p.wait("scalar", "sV", t_C)
        p.run("scalar", "activation", kn2[0:1, 2:4], kn2[0:1, 0:2], AF.Ln, inc="sS")
        p.wait("scalar", "sS", force=True)
        p.run("scalar", "activation", kn2[0:1, 4:6], kn2[0:1, 2:4], AF.Exp, scale=0.5,
              bias=sm1[0:1, SM_C:SM_C + 1], inc="sS")
        p.wait("vector", "sS")
        p.run("vector", "tensor_scalar", kn2[0:1, 6:8], kn2[0:1, 4:6], -1.0, None,
              op0=ALU.mult, inc="sV")
        p.wait("sync", "sS")
        if stage == "p0r":
            break
        bc_scalar(kn2[0:1, 6:7], smc[:, 4:5], 0)
        bc_scalar(kn2[0:1, 7:8], smc[:, 6:7], 1)
        t_negB = p.cnt["sV"]
        if pair == 0:
            tap("nb", kn2[:])
        p.wait("vector", "sT")
        p.run("vector", "tensor_mul", qtil[:], q_sb[:, pair * S:(pair + 1) * S], uneg[:], inc="sV")
        p.run("vector", "tensor_mul", qhat[:], q_sb[:, pair * S:(pair + 1) * S], upos[:], inc="sV")
        t_qtiles = p.cnt["sV"]
        if stage == "p0a":
            break

        for qc in range(4):
            qlo = qc * 512
            exp_tok = {}
            av_done = {}

            def emit_scores_exp(kc):
                sb_e = (kc % 2) * 2
                sb_o = sb_e + 1
                if kc >= 2:
                    p.wait("tensor", "sS", exp_tok[kc - 2])
                p.wait("tensor", "sV", t_qtiles)
                dlo, dhi = kc * 128, kc * 128 + 128
                for base, sb in ((0, sb_e), (64, sb_o)):
                    bo = sb * 512
                    pr = slice(base, base + 64)
                    segs = []
                    lo, hi = qlo, min(qlo + 512, dlo)
                    if hi > lo:
                        segs.append((lo, hi, "hat"))
                    lo, hi = max(qlo, dlo), min(qlo + 512, dhi)
                    if hi > lo:
                        segs.append((lo, hi, "diag"))
                    lo, hi = max(qlo, dhi), qlo + 512
                    if hi > lo:
                        segs.append((lo, hi, "til"))
                    for si, (lo, hi, kind) in enumerate(segs):
                        if kind == "hat":
                            lh = khat[pr, kc * 128:(kc + 1) * 128]
                            rh = qhat[pr, lo:hi]
                        elif kind == "til":
                            lh = ktil[pr, kc * 128:(kc + 1) * 128]
                            rh = qtil[pr, lo:hi]
                        else:
                            lh = kraw[pr, kc * 128:(kc + 1) * 128]
                            rh = q_sb[pr, pair * S + lo:pair * S + hi]
                        p.run("tensor", "matmul", psA[:, bo + (lo - qlo):bo + (hi - qlo)], lh, rh,
                              start=True, stop=True, tile_position=(base, 0),
                              inc="sT" if si == len(segs) - 1 else None)
                t_sc = p.cnt["sT"]
                if kc // 4 == qc:
                    dof = kc * 128 - qlo
                    p.wait("vector", "sT", t_sc)
                    p.run("vector", "tensor_mul", psA[:, sb_e * 512 + dof:sb_e * 512 + dof + 128],
                          psA[:, sb_e * 512 + dof:sb_e * 512 + dof + 128], dloc[:], inc="sV")
                    p.run("vector", "tensor_mul", psA[:, sb_o * 512 + dof:sb_o * 512 + dof + 128],
                          psA[:, sb_o * 512 + dof:sb_o * 512 + dof + 128], dloc[:], inc="sV")
                    p.wait("scalar", "sV")
                p.wait("scalar", "sT", t_sc)
                p.wait("scalar", "sV", t_negB)
                if kc >= 2:
                    p.wait("scalar", "sT", av_done[kc - 2])
                eb_e = expt[(kc % 2) * 2]
                eb_o = expt[(kc % 2) * 2 + 1]
                p.run("scalar", "activation", eb_e[:], psA[:, sb_e * 512:(sb_e + 1) * 512], AF.Exp,
                      bias=smc[:, 4:5], inc="sS")
                p.run("scalar", "activation", eb_o[:], psA[:, sb_o * 512:(sb_o + 1) * 512], AF.Exp,
                      bias=smc[:, 6:7], inc="sS")
                exp_tok[kc] = p.cnt["sS"]

            def emit_attnv(kc):
                eb_e = expt[(kc % 2) * 2]
                eb_o = expt[(kc % 2) * 2 + 1]
                p.wait("tensor", "sS", exp_tok[kc])
                if kc == 0:
                    p.wait("tensor", "sV", av_clear)
                p.run("tensor", "matmul", psA[0:65, 2048:2560],
                      vrec[:, kc * 1056 + he * 66:kc * 1056 + he * 66 + 65], eb_e[:],
                      start=(kc == 0), stop=(kc == 15))
                p.run("tensor", "matmul", psA[64:128, 2560:3072],
                      vrec[:, kc * 1056 + ho * 66:kc * 1056 + ho * 66 + 64], eb_o[:],
                      start=(kc == 0), stop=(kc == 15), tile_position=(0, 64))
                p.run("tensor", "matmul", psA[0:1, 3072:3584], ones1b[:], eb_o[:],
                      start=(kc == 0), stop=(kc == 15), tile_position=(0, 0), inc="sT")
                av_done[kc] = p.cnt["sT"]

            emit_scores_exp(0)
            for kc in range(1, 16):
                emit_scores_exp(kc)
                if stage != "p0b":
                    emit_attnv(kc - 1)
            if stage != "p0b":
                emit_attnv(15)
            else:
                for kc in range(16):
                    av_done[kc] = p.cnt["sT"]
            t_av = p.cnt["sT"]
            if stage == "p0b":
                av_clear = p.cnt["sV"]
                continue
            p.wait("vector", "sT", t_av)
            p.wait("vector", "qS")
            p.run("vector", "reciprocal", rcp_t[64:65, :], psA[64:65, 2048:2560], inc="sV")
            p.run("vector", "reciprocal", rcpO[:], psA[0:1, 3072:3584], inc="sV")
            trc = p.cnt["sV"]
            p.wait("sync", "sV", trc)
            dma("sync", rcpE0[:], rcp_t[64:65, :])
            bc(rcpE0[0:1, :], rb_t[:, 0:512], 512)
            bc(rcpO[0:1, :], rb_t[:, 512:1024], 512)
            trb = p.cnt["qS"]
            p.wait("vector", "qS", trb)
            aosl = ao_sb[:, pair * S + qlo:pair * S + qlo + 512]
            p.run("vector", "tensor_mul", aosl[0:64, :], psA[0:64, 2048:2560],
                  rb_t[0:64, 0:512], inc="sV")
            p.run("vector", "tensor_mul", aosl[64:128, :], psA[64:128, 2560:3072],
                  rb_t[64:128, 512:1024], inc="sV")
            av_clear = p.cnt["sV"]
        if pair == 0:
            tap("exp00", expt[0][:], bf=True)
    t_heads = p.cnt["sV"]
    tap("ao", ao_sb[:], bf=True)
    if stage in ("pair0", "heads", "p0a", "p0b", "p0k", "p0n", "p0r"):
        return finalize()

    p.wait("sync", "sV", t_heads)
    p.wait("sync", "sT")
    dma("sync", xB_t[:].rearrange("p (c t) -> p c t", c=4), xsp[:, 4:8, :])
    dma("sync", x_sb[:, 0:4 * S].rearrange("p (c t) -> p c t", c=4), xsp[:, 0:4, :])
    t_xrel = p.cnt["qS"]

    def xattn(c):
        if c < 4:
            return x_sb[:, c * S:(c + 1) * S]
        return xB_t[:, (c - 4) * S:(c - 4 + 1) * S]
    wo_evac = {}
    for fo in range(NCH):
        for n in range(4):
            idx = fo * 4 + n
            bank = idx % 2
            if idx >= 2:
                p.wait("tensor", "sV", wo_evac[idx - 2])
            p.wait("tensor", "qG", t_wo)
            p.wait("tensor", "sV", t_heads)
            for fi in range(NCH):
                p.run("tensor", "matmul", psA[:, bank * 512:(bank + 1) * 512],
                      wslice(wo_sb, D, fi, fo), ao_sb[:, fi * S + n * 512:fi * S + (n + 1) * 512],
                      start=(fi == 0), stop=(fi == NCH - 1), inc="sT" if fi == NCH - 1 else None)
            p.wait("vector", "sT")
            p.wait("vector", "qS", t_xrel)
            p.run("vector", "tensor_add", xattn(fo)[:, n * 512:(n + 1) * 512],
                  psA[:, bank * 512:(bank + 1) * 512], xattn(fo)[:, n * 512:(n + 1) * 512],
                  inc="sV")
            wo_evac[idx] = p.cnt["sV"]
    t_x2 = p.cnt["sV"]
    if stage == "wo":
        p.wait("sync", "sV", t_x2)
        dma("sync", x_sb[:, 4 * S:8 * S], xB_t[:])
        return finalize()

    p.wait("sync", "sV", t_x2)
    dma("sync", x_sb[:, 4 * S:8 * S], xB_t[:])
    t_xmove = p.cnt["qS"]
    tap("x2", x_sb[:])

    p.wait("gpsimd", "sV", t_heads)
    t_cngc2 = load_cngc()

    t_xfin = emit_ffn(xc, [("qS", t_xmove), ("sV", t_x2)], t_cngc2)

    return finalize()


_NC_CACHE = {}


def _get_nc(taps=()):
    key = tuple(sorted(taps))
    if key not in _NC_CACHE:
        _NC_CACHE[key] = build_program(taps=key)
    return _NC_CACHE[key]


_WMAP = {
    "wq": "L0_attn_W_q", "wk": "L0_attn_W_k", "wv": "L0_attn_W_v", "wo": "L0_attn_W_o",
    "wdn": "L0_attn_W_down_kv", "wuk": "L0_attn_W_up_k", "wuv": "L0_attn_W_up_v",
    "cw1": "L0_ffn_cn_w1", "cw2": "L0_ffn_cn_w2", "cw3": "L0_ffn_cn_w3", "cw4": "L0_ffn_cn_w4",
    "gw1": "L0_ffn_gc_w1", "gw2": "L0_ffn_gc_w2", "gw3": "L0_ffn_gc_w3",
    "cb1": "L0_ffn_cn_b1", "cb2": "L0_ffn_cn_b2", "cb3": "L0_ffn_cn_b3",
    "gb1": "L0_ffn_gc_b1", "gb2": "L0_ffn_gc_b2",
}
_SMAP = {"cb4": "L0_ffn_cn_b4", "gb3": "L0_ffn_gc_b3", "ar": "L0_ffn_adaptive_range"}


def _make_in_maps(inputs):
    x = np.asarray(inputs["L0_attn_x"])
    shared = {k: np.ascontiguousarray(np.asarray(inputs[v]), dtype=np.float32)
              for k, v in _WMAP.items()}
    for k, v in _SMAP.items():
        shared[k] = np.ascontiguousarray(np.asarray(inputs[v]), dtype=np.float32).reshape(1, 1)
    in_maps = []
    for b in range(8):
        m = dict(shared)
        m["x"] = np.ascontiguousarray(x[b].T.astype(np.float32))
        in_maps.append(m)
    return in_maps


def kernel(**inputs):
    nc = _get_nc()
    in_maps = _make_in_maps(inputs)
    res = run_bass_kernel_spmd(nc, in_maps, core_ids=list(range(8)))
    out = np.stack([np.asarray(res.results[b]["y"]).T for b in range(8)], axis=0)
    return np.ascontiguousarray(out.astype(np.float32))
